# version 37
# baseline (speedup 1.0000x reference)
import numpy as np
from contextlib import ExitStack

import concourse.bass as bass
import concourse.mybir as mybir
from concourse.bass_utils import run_bass_kernel_spmd

F32 = mybir.dt.float32
BF16 = mybir.dt.bfloat16
AF = mybir.ActivationFunctionType
ALU = mybir.AluOpType
AX = mybir.AxisListType

D = 1024
T = 2048
NH = 16
HD = 64
NCHUNK = 8
TCHUNK = 16
CLAMP_MAX = float(np.log(1.0 / 0.01))
C_MARGIN = 5.0
C_SIGMA = 3.9

ENGINES = ("sync", "gpsimd", "tensor", "scalar", "vector")


class Plan:
    def __init__(self):
        self.s = {e: [] for e in ENGINES}
        self.n = {}

    def op(self, eng, fn, waits=(), inc=None, amt=1):
        ws = tuple((s, v) for (s, v) in waits if v > 0)
        self.s[eng].append((fn, ws, inc, amt))
        if inc is not None:
            self.n[inc] = self.n.get(inc, 0) + amt
        return self.n.get(inc, 0) if inc is not None else 0

    def c(self, sem):
        return self.n.get(sem, 0)


def build_nc():
    nc = bass.Bass("TRN2", target_bir_lowering=False)

    xT = nc.dram_tensor("xT", [D, T], F32, kind="ExternalInput").ap()
    wq = nc.dram_tensor("wq", [D, D], F32, kind="ExternalInput").ap()
    wk = nc.dram_tensor("wk", [D, D], F32, kind="ExternalInput").ap()
    wv = nc.dram_tensor("wv", [D, D], F32, kind="ExternalInput").ap()
    wo = nc.dram_tensor("wo", [D, D], F32, kind="ExternalInput").ap()
    wff = nc.dram_tensor("wff", [D, D], F32, kind="ExternalInput").ap()
    wp = nc.dram_tensor("wp", [D, 256], F32, kind="ExternalInput").ap()
    wd = nc.dram_tensor("wd", [D, 15], F32, kind="ExternalInput").ap()
    wu = nc.dram_tensor("wu", [15, D], F32, kind="ExternalInput").ap()
    bp = nc.dram_tensor("bp", [1, 256], F32, kind="ExternalInput").ap()
    simT = nc.dram_tensor("simT", [4, 256], F32, kind="ExternalInput").ap()
    temp = nc.dram_tensor("temp", [1, 1], F32, kind="ExternalInput").ap()
    cos2 = nc.dram_tensor("cos2", [128, T], F32, kind="ExternalInput").ap()
    sin2 = nc.dram_tensor("sin2", [128, T], F32, kind="ExternalInput").ap()
    rt2 = nc.dram_tensor("rt2", [128, 128], F32, kind="ExternalInput").ap()
    ident = nc.dram_tensor("ident", [128, 128], F32, kind="ExternalInput").ap()
    eh = nc.dram_tensor("eh", [D, NH], F32, kind="ExternalInput").ap()
    outT = nc.dram_tensor("outT", [D, T], F32, kind="ExternalOutput").ap()

    scr_a = nc.dram_tensor("scr_a", [1, T], F32, kind="Internal").ap()
    scr_b = nc.dram_tensor("scr_b", [1, T], F32, kind="Internal").ap()
    scr_neg = nc.dram_tensor("scr_neg", [NH, T], BF16, kind="Internal").ap()
    scr_lin = nc.dram_tensor("scr_lin", [4, 1024], F32, kind="Internal").ap()
    scr_li2 = nc.dram_tensor("scr_li2", [4, 1024], BF16, kind="Internal").ap()

    es = ExitStack()
    S = lambda name, shape, dt: es.enter_context(nc.sbuf_tensor(name, shape, dt))
    xq = S("xq", [128, NCHUNK * T], BF16)
    qq = S("qq", [128, NCHUNK * T], BF16)
    kk = S("kk", [128, NCHUNK * T], BF16)
    vv = S("vv", [128, TCHUNK * 1071], BF16)
    gg = S("gg", [128, NCHUNK * T], BF16)
    wbf = S("wbf", [128, 8192], BF16)
    ff = S("ff", [128, T], F32)
    ggs = S("ggs", [128, 6144], BF16)
    hh = S("hh", [128, 4096], BF16)
    wd_bf = S("wd_bf", [128, NCHUNK * 15], BF16)
    identb = S("identb", [128, 128], BF16)
    rt2b = S("rt2b", [128, 128], BF16)
    ehb = S("ehb", [128, NCHUNK * NH], BF16)
    simtb = S("simtb", [4, 256], BF16)
    simn_c = S("simn_c", [128, 8], BF16)
    ones_col = S("ones_col", [128, 1], BF16)
    bp_col = S("bp_col", [128, 2], F32)
    r_tm = S("r_tm", [128, TCHUNK], F32)
    rmul_tm = S("rmul_tm", [128, TCHUNK], F32)
    temp_sb = S("temp_sb", [1, 1], F32)
    esc_sb = S("esc_sb", [1, 1], F32)
    eps1 = S("eps1", [1, 1], F32)
    epst = S("epst", [1, 1], F32)
    lrtm = S("lrtm", [128, 32], F32)
    lrtb = S("lrtb", [128, 32], BF16)
    rms16 = S("rms16", [16, 1], F32)
    rms16b = S("rms16b", [16, 1], F32)
    simnn = S("simnn", [4, 1], F32)

    PSA = es.enter_context(nc.psum_tensor("PSA", [128, T], F32))
    PSB = es.enter_context(nc.psum_tensor("PSB", [128, T], F32))

    sems = {}
    for sname in ("WD", "PE", "AC", "VE", "ST", "PO", "BN", "CN",
                  "WB", "WK", "WV", "WO", "WU", "WF", "WG", "NG0", "NG1",
                  "ST0", "ST1", "ST2", "ST3", "XA", "XB",
                  "X0", "X1", "X2", "X3", "L0", "L1", "L2", "L3"):
        sems[sname] = es.enter_context(nc.semaphore(sname))

    p = Plan()
    hf = hh[:].bitcast(F32)
    kf = kk[:].bitcast(F32)
    lt = hf[:, 0:64]
    p4g = hf[:, 64:128]
    mask = hf[:, 128:192]
    w4 = hf[:, 192:256]
    max_e = hf[:, 256:272]
    nm_e = hf[:, 272:288]
    se_e = hf[:, 288:304]
    rse_e = hf[:, 304:320]
    dtm = hh[:, 640:896]
    dwtm = hh[:, 896:1152]
    wu_bf = ggs[0:15, 0:1024]
    xc_chunk = lambda c: xq[:, c * T:(c + 1) * T]

    for c in range(NCHUNK):
        p.op("gpsimd", lambda e, c=c: e.dma_start(
            out=xq[:, c * T:(c + 1) * T],
            in_=xT[c * 128:(c + 1) * 128, :]),
            inc=f"X{c // 2}", amt=16)
    xw = lambda c: (f"X{c // 2}", 32)
    ck_wq = p.op("gpsimd", lambda e: e.dma_start(
        out=wbf[:].rearrange("p (c n) -> p c n", c=NCHUNK),
        in_=wq.rearrange("(c p) n -> p c n", p=128)), inc="WB", amt=16)
    p.op("gpsimd", lambda e: e.dma_start(out=identb[:], in_=ident),
         inc="CN", amt=16)
    p.op("gpsimd", lambda e: e.dma_start(out=rt2b[:], in_=rt2),
         inc="CN", amt=16)
    p.op("gpsimd", lambda e: e.dma_start(
        out=ehb[:].rearrange("p (c n) -> p c n", c=NCHUNK),
        in_=eh.rearrange("(c p) n -> p c n", p=128)), inc="CN", amt=16)
    p.op("gpsimd", lambda e: e.dma_start(out=simtb[:], in_=simT),
         inc="CN", amt=16)
    ck_cn = p.op("gpsimd", lambda e: e.dma_start(
        out=wd_bf[:].rearrange("p (c n) -> p c n", c=NCHUNK),
        in_=wd.rearrange("(c p) n -> p c n", p=128)), inc="CN", amt=16)
    ck_ident = ck_rt2 = ck_eh = ck_sim = ck_wd = ck_cn
    for oc2 in range(2):
        ck_bp = p.op("sync", lambda e, oc2=oc2: e.dma_start(
            out=bp_col[:, oc2:oc2 + 1],
            in_=bp[0:1, oc2 * 128:(oc2 + 1) * 128].rearrange(
                "o (pp c) -> (o pp) c", pp=128)),
            inc="WD", amt=16)
    ck_temp = p.op("sync", lambda e: e.dma_start(out=temp_sb[:], in_=temp),
                   inc="WD", amt=16)
    for tci in range(TCHUNK):
        p.op("vector", lambda e, tci=tci: e.memset(
            vv[:, tci * 1071: tci * 1071 + 1040].rearrange(
                "p (h d) -> p h d", d=65)[:, :, 64:65], 1.0))
        p.op("vector", lambda e, tci=tci: e.memset(
            vv[:, tci * 1071 + 1040: (tci + 1) * 1071], 0.0))
    p.op("vector", lambda e: e.memset(ones_col[:], 1.0))
    p.op("vector", lambda e: e.memset(eps1[:], 1e-5))
    ck_ones = p.op("vector", lambda e: e.memset(epst[:], 1e-24), inc="VE")

    sqv = ggs[:, 0:2048]
    SUMps = PSA[0:1, :]
    SSQps = PSB[0:1, :]
    pe_prev = 0
    for c in range(NCHUNK):
        ve_sq = p.op("vector", lambda e, c=c: e.tensor_tensor(
            sqv, xc_chunk(c), xc_chunk(c), ALU.mult),
            waits=[xw(c), ("PE", pe_prev),
                   ("VE", ck_ones if c == 0 else 0)], inc="VE")
        for j in range(4):
            p.op("tensor", lambda e, c=c, j=j: e.matmul(
                SUMps[:, j * 512:(j + 1) * 512], ones_col[:],
                xc_chunk(c)[:, j * 512:(j + 1) * 512],
                start=(c == 0), stop=(c == NCHUNK - 1), skip_group_check=True),
                waits=[xw(c), ("VE", ck_ones)]
                if j == 0 else [])
        for j in range(4):
            p.op("tensor", lambda e, c=c, j=j: e.matmul(
                SSQps[:, j * 512:(j + 1) * 512], ones_col[:],
                sqv[:, j * 512:(j + 1) * 512],
                start=(c == 0), stop=(c == NCHUNK - 1), skip_group_check=True),
                waits=[("VE", ve_sq)] if j == 0 else [],
                inc="PE" if j == 3 else None)
        pe_prev = p.c("PE")
    ck_stats_pe = p.c("PE")
    mean_r = hf[0:1, :]
    e2_r = kf[0:1, 0:2048]
    msq_r = kf[0:1, 2048:4096]
    ck_mean = p.op("scalar", lambda e: e.mul(mean_r, SUMps, 1.0 / D),
                   waits=[("PE", ck_stats_pe)], inc="AC")
    ck_e2 = p.op("scalar", lambda e: e.mul(e2_r, SSQps, 1.0 / D), inc="AC")
    mean_bc = ggs[:, 4096:6144]
    p.op("gpsimd", lambda e: e.dma_start(out=scr_a, in_=mean_r),
         waits=[("AC", ck_mean), ("BN", p.c("BN"))], inc="BN", amt=16)
    ck_meanbc = p.op("gpsimd", lambda e: e.dma_start(
        out=mean_bc, in_=scr_a.broadcast_to([128, T])),
        waits=[("BN", p.c("BN"))], inc="BN", amt=16)
    p.op("vector", lambda e: e.tensor_tensor(
        msq_r, mean_r, mean_r, ALU.mult), waits=[("AC", ck_mean)])
    p.op("vector", lambda e: e.drain())
    ck_var = p.op("vector", lambda e: e.tensor_tensor(
        msq_r, e2_r, msq_r, ALU.subtract), waits=[("AC", ck_e2)], inc="VE")
    ck_sroot = p.op("scalar", lambda e: e.activation(
        e2_r, msq_r, AF.Ln, bias=eps1[:]),
        waits=[("VE", ck_var), ("VE", ck_ones)], inc="AC")
    p.op("scalar", lambda e: e.drain())
    r_row = hf[0:1, :]
    p.op("scalar", lambda e: e.activation(
        r_row, e2_r, AF.Exp, scale=-0.5), waits=[("BN", ck_meanbc)])
    ck_rrow = p.op("scalar", lambda e: e.drain(), inc="AC")
    p.op("gpsimd", lambda e: e.dma_start(out=scr_b, in_=r_row),
         waits=[("AC", ck_rrow), ("BN", p.c("BN"))], inc="BN", amt=16)
    gd_rb = p.c("BN")
    ck_rbc = p.op("gpsimd", lambda e: e.dma_start(
        out=ff[:], in_=scr_b.broadcast_to([128, T])),
        waits=[("BN", gd_rb)], inc="BN", amt=16)
    for tc2 in range(TCHUNK):
        ck_rtm = p.op("sync", lambda e, tc2=tc2: e.dma_start(
            out=r_tm[:, tc2:tc2 + 1],
            in_=scr_b[0:1, tc2 * 128:(tc2 + 1) * 128].rearrange(
                "o (pp c) -> (o pp) c", pp=128)),
            waits=[("BN", gd_rb)] if tc2 == 0 else [], inc="WD", amt=16)
    ck_bp = ck_temp = ck_rtm

    for c in range(NCHUNK):
        p.op("vector", lambda e, c=c: e.tensor_tensor(
            xc_chunk(c), xc_chunk(c), mean_bc, ALU.subtract),
            waits=[("BN", ck_meanbc)] + ([("PE", ck_stats_pe)] if c == 0 else []),
            inc="VE")
    ck_xc_all = p.c("VE")

    cosr = ggs[:, 0:2048]
    sinr = ggs[:, 2048:4096]
    ck_ld_cos = p.op("gpsimd", lambda e: e.dma_start(out=mean_bc, in_=cos2),
                     waits=[("VE", ck_xc_all), ("PE", ck_stats_pe),
                            ("BN", p.c("BN"))], inc="BN", amt=16)
    ck_cosr = p.op("vector", lambda e: e.tensor_tensor(
        cosr, mean_bc, ff[:], ALU.mult),
        waits=[("BN", ck_ld_cos), ("BN", ck_rbc), ("PE", ck_stats_pe)],
        inc="VE")
    ck_ld_sin = p.op("gpsimd", lambda e: e.dma_start(out=mean_bc, in_=sin2),
                     waits=[("VE", ck_cosr), ("BN", p.c("BN"))],
                     inc="BN", amt=16)
    ck_sinr = p.op("vector", lambda e: e.tensor_tensor(
        sinr, mean_bc, ff[:], ALU.mult), waits=[("BN", ck_ld_sin)], inc="VE")
    ck_wk = p.op("gpsimd", lambda e: e.dma_start(
        out=gg[:, 0:8192].rearrange("p (c n) -> p c n", c=NCHUNK),
        in_=wk.rearrange("(c p) n -> p c n", p=128)), inc="WK", amt=16)

    def proj_fm(wview, dest_fn, extra_wait_first=()):
        evac_cks = []
        for oc in range(NCHUNK):
            PP = PSA if oc % 2 == 0 else PSB
            for kc in range(NCHUNK):
                for j in range(4):
                    w = []
                    if kc == 0 and j == 0:
                        if oc == 0:
                            w = list(extra_wait_first)
                        if oc >= 2:
                            w.append(("AC", evac_cks[oc - 2]))
                    is_last = (kc == NCHUNK - 1 and j == 3)
                    p.op("tensor", lambda e, oc=oc, kc=kc, j=j, PP=PP: e.matmul(
                        PP[:, j * 512:(j + 1) * 512], wview(kc, oc),
                        xc_chunk(kc)[:, j * 512:(j + 1) * 512],
                        start=(kc == 0), stop=(kc == NCHUNK - 1)),
                        waits=w, inc="PE" if is_last else None)
            last_pe = p.c("PE")
            evac_cks.append(p.op("scalar", lambda e, oc=oc, PP=PP: e.copy(
                dest_fn(oc), PP[:]), waits=[("PE", last_pe)], inc="AC"))
        return p.c("PE"), evac_cks

    wq_view = lambda kc, oc: wbf[:, kc * 1024 + oc * 128: kc * 1024 + oc * 128 + 128]
    ck_qpe, ck_qev = proj_fm(
        wq_view, lambda oc: qq[:, oc * T:(oc + 1) * T],
        extra_wait_first=[("WB", ck_wq), ("VE", ck_xc_all), ("AC", ck_sroot)])
    ck_wv = p.op("gpsimd", lambda e: e.dma_start(
        out=gg[:, 8192:16384].rearrange("p (c n) -> p c n", c=NCHUNK),
        in_=wv.rearrange("(c p) n -> p c n", p=128)), inc="WV", amt=16)
    ck_wp = p.op("gpsimd", lambda e: e.dma_start(
        out=wbf[:, 0:2048].rearrange("p (c n) -> p c n", c=NCHUNK),
        in_=wp.rearrange("(c p) n -> p c n", p=128)),
        waits=[("PE", ck_qpe)], inc="WB", amt=16)
    ck_wffa = p.op("gpsimd", lambda e: e.dma_start(
        out=wbf[:, 2048:8192].rearrange("p (c n) -> p c n", c=6),
        in_=wff.rearrange("(c p) n -> p c n", p=128)[:, 2:8, :]),
        inc="WF", amt=16)
    wk_view = lambda kc, oc: gg[:, kc * 1024 + oc * 128: kc * 1024 + oc * 128 + 128]

    rqbA = ggs[:, 4096:5120]
    t1A = ggs[:, 5120:6144]
    t2A = hh[:, 0:1024]
    rqbB = hh[:, 1024:2048]
    t1B = hh[:, 2048:3072]
    t2B = hh[:, 3072:4096]
    rope_ve = {}
    RQh = [PSA[:, 0:1024], PSA[:, 1024:2048]]
    vslots = [PSB[:, 0:1024], PSB[:, 1024:2048]]
    ck_vev = []
    ck_kev_pc = []
    rope_plan = []
    for src, evs in ((qq, ck_qev), (kk, ck_kev_pc)):
        for c in range(NCHUNK):
            for h2 in range(2):
                rope_plan.append((src, evs, c, h2))

    def emit_rope_pair(ih):
        sA, eA, cA, hA = rope_plan[ih]
        sB, eB, cB, hB = rope_plan[ih + 1]
        slA = slice(cA * T + hA * 1024, cA * T + (hA + 1) * 1024)
        slB = slice(cB * T + hB * 1024, cB * T + (hB + 1) * 1024)
        psA = slice(hA * 1024, (hA + 1) * 1024)
        psB = slice(hB * 1024, (hB + 1) * 1024)
        for k, (src, evs, c, RQ, sl) in enumerate(
                ((sA, eA, cA, RQh[0], slA), (sB, eB, cB, RQh[1], slB))):
            w0 = [("AC", evs[c])]
            if ih == 0 and k == 0:
                w0 += [("AC", ck_qev[7]), ("CN", ck_rt2)]
            if ih >= 2:
                w0.append(("VE", rope_ve[ih - 2]))
            for j in range(2):
                p.op("tensor", lambda e, src=src, sl=sl, j=j, RQ=RQ: e.matmul(
                    RQ[:, j * 512:(j + 1) * 512], rt2b[:],
                    src[:, sl.start + j * 512: sl.start + (j + 1) * 512],
                    start=True, stop=True),
                    waits=w0 if j == 0 else [], inc="PE" if j == 1 else None)
        pe_rq = p.c("PE")
        p.op("vector", lambda e: e.tensor_copy(rqbA, RQh[0][:]),
             waits=[("PE", pe_rq)] + ([("VE", ck_sinr)] if ih == 0 else []))
        p.op("vector", lambda e: e.tensor_copy(rqbB, RQh[1][:]))
        p.op("vector", lambda e, sA=sA, slA=slA, psA=psA: e.tensor_tensor(
            t1A, sA[:, slA], cosr[:, psA], ALU.mult))
        p.op("vector", lambda e, sB=sB, slB=slB, psB=psB: e.tensor_tensor(
            t1B, sB[:, slB], cosr[:, psB], ALU.mult))
        p.op("vector", lambda e: e.drain())
        p.op("vector", lambda e, psA=psA: e.tensor_tensor(
            t2A, rqbA, sinr[:, psA], ALU.mult))
        p.op("vector", lambda e, psB=psB: e.tensor_tensor(
            t2B, rqbB, sinr[:, psB], ALU.mult))
        p.op("vector", lambda e: e.drain())
        p.op("vector", lambda e, sA=sA, slA=slA: e.tensor_tensor(
            sA[:, slA], t1A, t2A, ALU.add))
        rope_ve[ih] = rope_ve[ih + 1] = p.op(
            "vector", lambda e, sB=sB, slB=slB: e.tensor_tensor(
                sB[:, slB], t1B, t2B, ALU.add), inc="VE")

    def emit_v_chunk(tc):
        PV = vslots[tc % 2]
        for kc in range(NCHUNK):
            for j in range(2):
                w = []
                if kc == 0 and j == 0:
                    if tc == 0:
                        w = [("WV", ck_wv), ("AC", kev_h[-1])]
                    if tc >= 2:
                        w.append(("AC", ck_vev[tc - 2]))
                p.op("tensor", lambda e, tc=tc, kc=kc, j=j, PV=PV: e.matmul(
                    PV[:, j * 512:(j + 1) * 512],
                    xc_chunk(kc)[:, tc * 128:(tc + 1) * 128],
                    gg[:, 8192 + kc * 1024 + j * 512: 8192 + kc * 1024 + (j + 1) * 512],
                    start=(kc == 0), stop=(kc == NCHUNK - 1)),
                    waits=w,
                    inc="PE" if (kc == NCHUNK - 1 and j == 1) else None)
        pe_v = p.c("PE")
        ck_vev.append(p.op("scalar", lambda e, tc=tc, PV=PV: e.mul(
            vv[:, tc * 1071: tc * 1071 + 1040].rearrange(
                "p (h d) -> p h d", d=65)[:, :, 0:64],
            PV[:].rearrange("p (h d) -> p h d", d=64),
            r_tm[:, tc:tc + 1]),
            waits=[("PE", pe_v), ("WD", ck_rtm)], inc="AC"))

    kev_h = []
    for oc in range(NCHUNK):
        for h in range(2):
            hu = 2 * oc + h
            PU = PSB[:, h * 1024:(h + 1) * 1024]
            for kc in range(NCHUNK):
                for j in range(2):
                    w = []
                    if kc == 0 and j == 0:
                        if hu == 0:
                            w = [("WK", ck_wk), ("AC", ck_qev[7]),
                                 ("AC", ck_rrow)]
                        if hu >= 2:
                            w.append(("AC", kev_h[hu - 2]))
                    p.op("tensor",
                         lambda e, oc=oc, h=h, kc=kc, j=j, PU=PU: e.matmul(
                             PU[:, j * 512:(j + 1) * 512], wk_view(kc, oc),
                             xc_chunk(kc)[:, h * 1024 + j * 512:
                                          h * 1024 + (j + 1) * 512],
                             start=(kc == 0), stop=(kc == NCHUNK - 1)),
                         waits=w,
                         inc="PE" if (kc == NCHUNK - 1 and j == 1) else None)
            pe_h = p.c("PE")
            kev_h.append(p.op("scalar", lambda e, oc=oc, h=h, PU=PU: e.copy(
                kk[:, oc * T + h * 1024: oc * T + (h + 1) * 1024], PU),
                waits=[("PE", pe_h)], inc="AC"))
        ck_kev_pc.append(kev_h[-1])
        emit_rope_pair(2 * oc)
    for tc in range(TCHUNK):
        emit_v_chunk(tc)
        if tc % 2 == 0:
            emit_rope_pair(16 + tc)
    ck_rope_all = p.c("VE")
    ck_v_done = p.c("AC")

    p.op("vector", lambda e: e.drain())
    sqscrs = [ggs[:, 4096:6144], ggs[:, 0:2048]]
    SSQQ = PSA[0:16, :]
    SSK = PSB[0:16, :]
    pe_done = {}
    ci = 0
    for src, SS in ((qq, SSQQ), (kk, SSK)):
        for c in range(NCHUNK):
            sq_t = sqscrs[ci % 2]
            if ci % 3 == 2:
                sq_ck = p.op("scalar", lambda e, src=src, c=c, sq_t=sq_t:
                             e.activation(sq_t, src[:, c * T:(c + 1) * T],
                                          AF.Square),
                             waits=[("PE", pe_done.get(ci - 2, 0))], inc="AC")
                wgate = ("AC", sq_ck)
            else:
                sq_ck = p.op("vector", lambda e, src=src, c=c, sq_t=sq_t:
                             e.tensor_tensor(
                                 sq_t, src[:, c * T:(c + 1) * T],
                                 src[:, c * T:(c + 1) * T], ALU.mult),
                             waits=[("PE", pe_done.get(ci - 2, 0))], inc="VE")
                wgate = ("VE", sq_ck)
            for j in range(4):
                p.op("tensor", lambda e, c=c, j=j, SS=SS, sq_t=sq_t: e.matmul(
                    SS[:, j * 512:(j + 1) * 512], ehb[:, c * NH:(c + 1) * NH],
                    sq_t[:, j * 512:(j + 1) * 512],
                    start=(c == 0), stop=(c == NCHUNK - 1),
                    skip_group_check=True),
                    waits=([wgate] +
                           ([("CN", ck_eh), ("AC", ck_v_done)] if (c == 0 and ci < 2) else []))
                    if j == 0 else [],
                    inc="PE" if j == 3 else None)
            pe_done[ci] = p.c("PE")
            ci += 1
    ck_cm_pe = p.c("PE")
    ck_sqq = p.op("scalar", lambda e: e.activation(
        hh[0:16, 0:2048], SSQQ, AF.Sqrt),
        waits=[("PE", pe_done[NCHUNK - 1])], inc="AC")
    ck_sks = p.op("vector", lambda e: e.tensor_reduce(
        rms16[:], SSK, axis=AX.X, op=ALU.add),
        waits=[("PE", ck_cm_pe)], inc="VE")
    ck_rms = p.op("scalar", lambda e: e.activation(
        rms16b[:], rms16[:], AF.Sqrt, scale=1.0 / (T * HD)),
        waits=[("VE", ck_sks)], inc="AC")
    negc = ggs[0:16, 2048:4096]
    p.op("vector", lambda e: e.tensor_scalar(
        negc, hh[0:16, 0:2048], rms16b[:], -C_SIGMA, ALU.mult, ALU.mult),
        waits=[("AC", ck_rms), ("AC", ck_sqq)])
    p.op("vector", lambda e: e.drain())
    p.op("vector", lambda e: e.tensor_scalar(
        negc, negc, -C_MARGIN, None, ALU.add))
    ck_negc_ve = p.op("vector", lambda e: e.drain(), inc="VE")
    ck_negc = p.op("gpsimd", lambda e: e.dma_start(out=scr_neg, in_=negc),
                   waits=[("VE", ck_negc_ve), ("BN", p.c("BN"))],
                   inc="BN", amt=16)

    wp_view = lambda kc, oc: wbf[:, kc * 256 + oc * 128: kc * 256 + oc * 128 + 128]
    pbf = [ggs[:, 0:2048], ggs[:, 2048:4096]]
    ck_pb = []
    ve_pm = []
    for oc in range(2):
        for kc in range(NCHUNK):
            for j in range(4):
                w = []
                if kc == 0 and j == 0:
                    w = [("WB", ck_wp), ("AC", ck_v_done), ("VE", ck_sks)]
                    if oc == 1:
                        w.append(("VE", ve_pm[0]))
                p.op("tensor", lambda e, oc=oc, kc=kc, j=j: e.matmul(
                    PSB[:, j * 512:(j + 1) * 512], wp_view(kc, oc),
                    xc_chunk(kc)[:, j * 512:(j + 1) * 512],
                    start=(kc == 0), stop=(kc == NCHUNK - 1)),
                    waits=w,
                    inc="PE" if (kc == NCHUNK - 1 and j == 3) else None)
        pe_p = p.c("PE")
        ve_pm.append(p.op("vector", lambda e, oc=oc: e.tensor_tensor(
            pbf[oc], PSB[:], ff[:], ALU.mult),
            waits=[("PE", pe_p), ("BN", ck_rbc)] +
                  ([("VE", ck_negc_ve)] if oc == 0 else
                   [("BN", ck_negc)]), inc="VE"))
        ck_pb.append(p.op("scalar", lambda e, oc=oc: e.add(
            pbf[oc], pbf[oc], bp_col[:, oc:oc + 1]),
            waits=[("VE", ve_pm[-1]), ("WD", ck_bp)], inc="AC"))
    ck_wffb = p.op("gpsimd", lambda e: e.dma_start(
        out=wbf[:, 0:2048].rearrange("p (c n) -> p c n", c=2),
        in_=wff.rearrange("(c p) n -> p c n", p=128)[:, 0:2, :]),
        waits=[("PE", p.c("PE"))], inc="WG", amt=16)

    wff_view = lambda kc, oc: wbf[:, kc * 1024 + oc * 128: kc * 1024 + oc * 128 + 128]
    KCORD = [2, 3, 4, 5, 6, 7, 0, 1]
    p4a_ev = {}
    p4a_pe = {}

    def p4a_mms(u):
        oc, h = u // 2, u % 2
        PU = PSA[:, h * 1024:(h + 1) * 1024]
        for ki, kc in enumerate(KCORD):
            for j in range(2):
                w = []
                if ki == 0 and j == 0:
                    if u == 0:
                        w = [("WF", ck_wffa), ("AC", ck_sqq)]
                    if u >= 2:
                        w.append(("AC", p4a_ev[u - 2]))
                if kc == 0 and j == 0 and u == 0:
                    w.append(("WG", ck_wffb))
                p.op("tensor",
                     lambda e, oc=oc, h=h, kc=kc, j=j, PU=PU, ki=ki: e.matmul(
                         PU[:, j * 512:(j + 1) * 512], wff_view(kc, oc),
                         xc_chunk(kc)[:, h * 1024 + j * 512:
                                      h * 1024 + (j + 1) * 512],
                         start=(ki == 0), stop=(ki == NCHUNK - 1)),
                     waits=w,
                     inc="PE" if (ki == NCHUNK - 1 and j == 1) else None)
        p4a_pe[u] = p.c("PE")

    def p4a_evac(u):
        oc, h = u // 2, u % 2
        PU = PSA[:, h * 1024:(h + 1) * 1024]
        p4a_ev[u] = p.op("scalar", lambda e, oc=oc, h=h, PU=PU: e.copy(
            gg[:, oc * T + h * 1024: oc * T + (h + 1) * 1024], PU),
            waits=[("PE", p4a_pe[u])], inc="AC")

    p4a_mms(0)
    p4a_mms(1)
    p4a_evac(0)
    p4a_mms(2)
    p4a_evac(1)
    p4a_mms(3)
    SSP = PSB[0:1, :]
    sq2 = ggs[:, 4096:6144]
    pe_prev = p.c("PE")
    for oc in range(2):
        ve_sq = p.op("vector", lambda e, oc=oc: e.tensor_tensor(
            sq2, pbf[oc], pbf[oc], ALU.mult),
            waits=[("AC", ck_pb[oc]), ("PE", pe_prev)], inc="VE")
        for j in range(4):
            p.op("tensor", lambda e, oc=oc, j=j: e.matmul(
                SSP[:, j * 512:(j + 1) * 512], ones_col[:],
                sq2[:, j * 512:(j + 1) * 512],
                start=(oc == 0), stop=(oc == 1), skip_group_check=True),
                waits=[("VE", ve_sq)] if j == 0 else [],
                inc="PE" if j == 3 else None)
        pe_prev = p.c("PE")
    ck_ssp_pe = p.c("PE")
    p4a_evac(2)
    p4a_mms(4)
    simsq = hf[0:4, 0:256]
    ck_simsq = p.op("scalar", lambda e: e.activation(
        simsq, simtb[:], AF.Square), waits=[("CN", ck_sim)], inc="AC")
    ck_simn1 = p.op("vector", lambda e: e.tensor_reduce(
        simnn[:], simsq, axis=AX.X, op=ALU.add),
        waits=[("AC", ck_simsq)], inc="VE")
    ck_simn2 = p.op("scalar", lambda e: e.activation(
        simnn[:], simnn[:], AF.Sqrt), waits=[("VE", ck_simn1)], inc="AC")
    p.op("vector", lambda e: e.reciprocal(simnn[:], simnn[:]),
         waits=[("AC", ck_simn2)])
    p.op("vector", lambda e: e.drain())
    ve_simn = p.op("vector", lambda e: e.tensor_scalar(
        simtb[:], simtb[:], simnn[:], None, ALU.mult), inc="VE")
    rmul_r = hf[0:1, :]
    ck_lnp = p.op("scalar", lambda e: e.activation(
        rmul_r, SSP, AF.Ln, bias=epst[:]),
        waits=[("PE", ck_ssp_pe), ("VE", ck_simn1)], inc="AC")
    p.op("scalar", lambda e: e.drain())
    ck_sqrtp = p.op("scalar", lambda e: e.activation(
        rmul_r, rmul_r, AF.Exp, scale=-0.5), inc="AC")
    p4a_evac(3)
    p.op("vector", lambda e: e.tensor_scalar(
        esc_sb[:], temp_sb[:], CLAMP_MAX, None, ALU.min),
        waits=[("WD", ck_temp)])
    ve_esc = p.op("vector", lambda e: e.drain(), inc="VE")
    ck_esc = p.op("scalar", lambda e: e.activation(
        esc_sb[:], esc_sb[:], AF.Exp), waits=[("VE", ve_esc)], inc="AC")
    p.op("vector", lambda e: e.tensor_scalar(
        rmul_r, rmul_r, esc_sb[:], None, ALU.mult),
        waits=[("AC", ck_esc), ("AC", ck_sqrtp)])
    ck_rmul = p.op("vector", lambda e: e.drain(), inc="VE")
    PSpt = PSB[:].bitcast(BF16)[:, 0:8]
    for half in range(2):
        p.op("tensor", lambda e, half=half: e.transpose(
            PSpt[:, half * 4:(half + 1) * 4],
            simtb[:, half * 128:(half + 1) * 128], identb[0:4, 0:4]),
            waits=[("VE", ve_simn), ("CN", ck_ident),
                   ("AC", ck_lnp)] if half == 0 else [],
            inc="PE" if half == 1 else None)
    ck_simn_pe = p.c("PE")
    p4a_mms(5)
    p.op("vector", lambda e: e.tensor_copy(simn_c[:], PSpt),
         waits=[("PE", ck_simn_pe)])
    ck_simn_c = p.op("vector", lambda e: e.drain(), inc="VE")
    p4a_evac(4)
    LU = PSB[0:4, :]
    for oc in range(2):
        for j in range(4):
            p.op("tensor", lambda e, oc=oc, j=j: e.matmul(
                LU[:, j * 512:(j + 1) * 512],
                simn_c[:, oc * 4:(oc + 1) * 4],
                pbf[oc][:, j * 512:(j + 1) * 512],
                start=(oc == 0), stop=(oc == 1), skip_group_check=True),
                waits=[("VE", ck_simn_c),
                       ("AC", ck_sqrtp)] if (oc == 0 and j == 0) else [],
                inc="PE" if (oc == 1 and j == 3) else None)
    ck_lu_pe = p.c("PE")
    p4a_mms(6)
    p4a_evac(5)
    ck_wu = p.op("gpsimd", lambda e: e.dma_start(out=wu_bf, in_=wu),
                 waits=[("PE", ck_lu_pe)], inc="WU", amt=16)
    p.op("gpsimd", lambda e: e.dma_start(out=scr_a, in_=rmul_r),
         waits=[("VE", ck_rmul), ("BN", p.c("BN"))], inc="BN", amt=16)
    gd_ra = p.c("BN")
    ck_rmt = 0
    for tc2 in range(TCHUNK):
        ck_rmt = p.op("sync", lambda e, tc2=tc2: e.dma_start(
            out=rmul_tm[:, tc2:tc2 + 1],
            in_=scr_a[0:1, tc2 * 128:(tc2 + 1) * 128].rearrange(
                "o (pp c) -> (o pp) c", pp=128)),
            waits=[("BN", gd_ra)] if tc2 == 0 else [], inc="WD", amt=16)
    lsc = ggs[0:4, 4096:6144]
    ck_lsc_a = p.op("scalar", lambda e: e.copy(lsc, LU),
                    waits=[("PE", ck_lu_pe)], inc="AC")
    PSlt = PSB[:].bitcast(BF16)[:, 0:64]
    for j in range(TCHUNK):
        p.op("tensor", lambda e, j=j: e.transpose(
            PSlt[:, j * 4:(j + 1) * 4], lsc[:, j * 128:(j + 1) * 128],
            identb[0:4, 0:4]),
            waits=[("AC", ck_lsc_a)] if j == 0 else [],
            inc="PE" if j == TCHUNK - 1 else None)
    ck_lt_pe = p.c("PE")
    p4a_mms(7)
    p4a_evac(6)
    p.op("vector", lambda e: e.tensor_copy(lt, PSlt),
         waits=[("PE", ck_lt_pe), ("BN", gd_ra)])
    p.op("vector", lambda e: e.drain())
    for j in range(TCHUNK):
        p.op("vector", lambda e, j=j: e.tensor_scalar(
            lt[:, j * 4:(j + 1) * 4], lt[:, j * 4:(j + 1) * 4],
            rmul_tm[:, j:j + 1], None, ALU.mult),
            waits=[("WD", ck_rmt)] if j == 0 else [])
    ck_lt = p.op("vector", lambda e: e.drain(), inc="VE")
    for j in range(TCHUNK):
        p.op("vector", lambda e, j=j: e.tensor_reduce(
            max_e[:, j:j + 1], lt[:, j * 4:(j + 1) * 4], axis=AX.X, op=ALU.max))
    p.op("vector", lambda e: e.drain())
    p.op("vector", lambda e: e.tensor_scalar(
        nm_e, max_e, -1.0, None, ALU.mult))
    ve_nm = p.op("vector", lambda e: e.drain(), inc="VE")
    p4a_mms(8)
    p4a_evac(7)
    ck_exps = 0
    for j in range(TCHUNK):
        ck_exps = p.op("scalar", lambda e, j=j: e.activation(
            p4g[:, j * 4:(j + 1) * 4], lt[:, j * 4:(j + 1) * 4], AF.Exp,
            bias=nm_e[:, j:j + 1]),
            waits=[("VE", ve_nm)] if j == 0 else [], inc="AC")
    for j in range(TCHUNK):
        p.op("vector", lambda e, j=j: e.tensor_reduce(
            se_e[:, j:j + 1], p4g[:, j * 4:(j + 1) * 4], axis=AX.X, op=ALU.add),
            waits=[("AC", ck_exps)] if j == 0 else [])
    p.op("vector", lambda e: e.drain())
    p.op("vector", lambda e: e.reciprocal(rse_e, se_e))
    p.op("vector", lambda e: e.drain())
    for j in range(TCHUNK):
        p.op("vector", lambda e, j=j: e.tensor_scalar(
            mask[:, j * 4:(j + 1) * 4], lt[:, j * 4:(j + 1) * 4],
            max_e[:, j:j + 1], None, ALU.is_ge))
    p.op("vector", lambda e: e.drain())
    p.op("vector", lambda e: e.tensor_tensor(w4, mask, p4g, ALU.mult))
    p.op("vector", lambda e: e.drain())
    for j in range(TCHUNK):
        p.op("vector", lambda e, j=j: e.tensor_scalar(
            w4[:, j * 4:(j + 1) * 4], w4[:, j * 4:(j + 1) * 4],
            rse_e[:, j:j + 1], None, ALU.mult))
    ck_w4 = p.op("vector", lambda e: e.drain(), inc="VE")
    p4a_mms(9)
    p4a_evac(8)

    DW = PSB[0:15, :]
    for kc in range(NCHUNK):
        for j in range(4):
            p.op("tensor", lambda e, kc=kc, j=j: e.matmul(
                DW[:, j * 512:(j + 1) * 512],
                wd_bf[:, kc * 15:(kc + 1) * 15],
                xc_chunk(kc)[:, j * 512:(j + 1) * 512],
                start=(kc == 0), stop=(kc == NCHUNK - 1)),
                waits=[("CN", ck_wd), ("VE", ck_lt)] if (kc == 0 and j == 0) else [],
                inc="PE" if (kc == NCHUNK - 1 and j == 3) else None)
    ck_dw_pe = p.c("PE")
    p4a_mms(10)
    down_sb = ggs[0:15, 4096:6144]
    ck_down = p.op("scalar", lambda e: e.copy(down_sb, DW),
                   waits=[("PE", ck_dw_pe)], inc="AC")
    p4a_evac(9)
    PSdt = PSB[:].bitcast(BF16)[:, 64:320]
    for j in range(TCHUNK):
        p.op("tensor", lambda e, j=j: e.transpose(
            PSdt[:, j * 16: j * 16 + 15], down_sb[:, j * 128:(j + 1) * 128],
            identb[0:15, 0:15]),
            waits=[("AC", ck_down), ("VE", ck_lt)] if j == 0 else [],
            inc="PE" if j == TCHUNK - 1 else None)
    ck_dt_pe = p.c("PE")
    p4a_mms(11)
    p.op("vector", lambda e: e.tensor_copy(dtm, PSdt),
         waits=[("PE", ck_dt_pe), ("BN", gd_ra)])
    ck_dtm = p.op("vector", lambda e: e.drain(), inc="VE")
    p4a_evac(10)
    offs = [0, 1, 3, 7, 15]
    for j in range(TCHUNK):
        for ei in range(4):
            a, b = offs[ei], offs[ei + 1]
            p.op("vector", lambda e, j=j, ei=ei, a=a, b=b: e.tensor_scalar(
                dwtm[:, j * 16 + a: j * 16 + b], dtm[:, j * 16 + a: j * 16 + b],
                w4[:, j * 4 + ei: j * 4 + ei + 1], None, ALU.mult),
                waits=[("VE", ck_w4), ("VE", ck_dtm)] if (j == 0 and ei == 0) else [])
    ve_dwtm = p.op("vector", lambda e: e.drain(), inc="VE")
    p4a_mms(12)
    p4a_evac(11)
    p4a_mms(13)
    PSdw = PSB[:].bitcast(BF16)[0:15, 0:2048]
    for j in range(TCHUNK):
        p.op("tensor", lambda e, j=j: e.transpose(
            PSdw[:, j * 128:(j + 1) * 128], dwtm[:, j * 16: j * 16 + 15],
            identb[:, :]),
            waits=[("VE", ve_dwtm)] if j == 0 else [],
            inc="PE" if j == TCHUNK - 1 else None)
    ck_dw2_pe = p.c("PE")
    downw = ggs[0:15, 2048:4096]
    p.op("vector", lambda e: e.tensor_copy(downw, PSdw),
         waits=[("PE", ck_dw2_pe)])
    p.op("vector", lambda e: e.drain())
    p.op("vector", lambda e: e.tensor_tensor(
        downw, downw, ff[0:15, :], ALU.mult))
    ck_downw = p.op("vector", lambda e: e.drain(), inc="VE")
    p4a_evac(12)
    p4a_mms(14)
    p4a_evac(13)
    p4a_mms(15)
    p4a_evac(14)
    p4a_evac(15)

    bt = {}
    for u in range(16):
        oc, h = u // 2, u % 2
        bt[u] = p.op("vector", lambda e, oc=oc, h=h: e.tensor_tensor(
            gg[:, oc * T + h * 1024: oc * T + (h + 1) * 1024],
            gg[:, oc * T + h * 1024: oc * T + (h + 1) * 1024],
            ff[:, h * 1024:(h + 1) * 1024], ALU.mult),
            waits=[("AC", p4a_ev[u])], inc="VE")
    ck_g_sil = {}
    for u in range(16):
        oc, h = u // 2, u % 2
        PU = PSA[:, h * 1024:(h + 1) * 1024]
        for j in range(2):
            w = []
            if j == 0:
                w = [("VE", ck_downw)]
                if u < 2:
                    w += [("WU", ck_wu), ("AC", p4a_ev[14 + h])]
                if u >= 2:
                    w.append(("AC", ck_g_sil[u - 2]))
            p.op("tensor", lambda e, oc=oc, h=h, j=j, PU=PU: e.matmul(
                PU[:, j * 512:(j + 1) * 512],
                wu_bf[:, oc * 128:(oc + 1) * 128],
                downw[:, h * 1024 + j * 512: h * 1024 + (j + 1) * 512],
                start=True, stop=False), waits=w)
        for j in range(2):
            p.op("tensor", lambda e, oc=oc, h=h, j=j, PU=PU: e.matmul(
                PU[:, j * 512:(j + 1) * 512], identb[:],
                gg[:, oc * T + h * 1024 + j * 512:
                    oc * T + h * 1024 + (j + 1) * 512],
                start=False, stop=True),
                waits=[("VE", bt[u])] if j == 0 else [],
                inc="PE" if j == 1 else None)
        pe_u = p.c("PE")
        ck_g_sil[u] = p.op("scalar", lambda e, oc=oc, h=h, PU=PU: e.activation(
            gg[:, oc * T + h * 1024: oc * T + (h + 1) * 1024], PU, AF.Silu),
            waits=[("PE", pe_u)], inc="AC")
    ck_g_done_ac = p.c("AC")
    ck_g_done_ve = p.c("VE")

    SSs = [PSA[:, 0:1024], PSA[:, 1024:2048]]
    AVs = [PSB[:, 0:1024], PSB[:, 1024:2048]]
    Pps = [ggs[:, 4096:5120], ggs[:, 5120:6144],
           wbf[:, 4096:5120], wbf[:, 5120:6144]]
    wbfF = wbf[:].bitcast(F32)
    strips = [wbfF[0:65, 0:1024], wbfF[0:65, 1024:2048]]
    qaug = [ggs[0:96, 0:2048], ggs[0:96, 2048:4096]]
    kaug = [hh[0:96, 0:2048], hh[0:96, 2048:4096]]
    ve_pad = 0
    for b in range(2):
        p.op("vector", lambda e, b=b: e.memset(
            ggs[64:96, b * 2048:(b + 1) * 2048], 0.0),
            waits=[("PE", ck_lu_pe), ("VE", ck_downw)] if b == 0 else [])
        p.op("vector", lambda e, b=b: e.memset(
            hh[64:96, b * 2048:(b + 1) * 2048], 0.0),
            waits=[("PE", ck_dw2_pe)] if b == 0 else [])
        p.op("vector", lambda e: e.drain())
        ve_pad = p.op("vector", lambda e, b=b: e.memset(
            hh[64:65, b * 2048:(b + 1) * 2048], 1.0), inc="VE")

    qk_pe_after_head = {}
    build_ve = {}
    build_gd = {}
    evac_ve = {}
    lazy_tt_ve = {}
    lazy_gd1 = {}
    pe_av = {}
    ac_exp = {}
    ck_wo = 0
    g_iter = 0
    W71 = 1071
    ffb = ff[:].bitcast(BF16)
    evac_ac = {}
    strip_users = [[], []]
    pending_av = []
    lazyq = []
    lazy_done_ve = {}
    nonlocal_po = [0]
    ve_sh0 = 0

    def flush_one_av():
        g, hx, t, AV, Pp, w_av = pending_av.pop(0)
        for j in range(2):
            p.op("tensor", lambda e, hx=hx, t=t, j=j, AV=AV, Pp=Pp: e.matmul(
                AV[0:96, j * 512:(j + 1) * 512],
                vv[:, t * W71 + 65 * hx: t * W71 + 65 * hx + 96],
                Pp[:, j * 512:(j + 1) * 512],
                start=(t == 0), stop=(t == TCHUNK - 1)),
                waits=w_av if j == 0 else [], inc="PE" if j == 1 else None)
        pe_av[g] = p.c("PE")

    def flush_to(depth):
        while len(pending_av) > depth:
            flush_one_av()

    def emit_builds(h, w_extra, w_q=()):
        par = h % 2
        hc, hr = h // 2, 64 * (h % 2)
        build_gd[h] = p.op("gpsimd", lambda e, par=par, h=h: e.dma_start(
            out=qaug[par][64:65, :], in_=scr_neg[h:h + 1, :]),
            waits=[("BN", ck_negc), ("VE", ve_pad)] + w_extra,
            inc=f"NG{par}", amt=16)
        p.op("vector", lambda e, par=par, hr=hr, hc=hc: e.tensor_copy(
            kaug[par][0:64, :], kk[hr:hr + 64, hc * T:(hc + 1) * T]),
            waits=w_extra + [("VE", ve_pad)])
        p.op("vector", lambda e, par=par, hr=hr, hc=hc: e.tensor_copy(
            qaug[par][0:64, :], qq[hr:hr + 64, hc * T:(hc + 1) * T]),
            waits=list(w_q))
        build_ve[h] = p.op("vector", lambda e: e.drain(), inc="VE")

    def emit_lazy_front(ei, hx, sh):
        si = ei % 4
        L = f"L{si}"
        lazy_gd1[si] = p.op("gpsimd", lambda e, si=si, ei=ei: e.dma_start(
            out=scr_lin[si:si + 1, :], in_=strips[ei % 2][64:65, :]),
            waits=[("VE", evac_ac[(hx, sh)])], inc=L, amt=16)
        p.op("gpsimd", lambda e, si=si: e.dma_start(
            out=lrtm[:, si * 8:(si + 1) * 8],
            in_=scr_lin[si:si + 1, :].rearrange("o (pp c) -> (o pp) c", pp=128)),
            waits=[(L, lazy_gd1[si])], inc=L, amt=16)
        lazyq.append((ei, hx, sh, si, p.c(L)))

    def emit_lazy_back():
        ei, hx, sh, si, gd2 = lazyq.pop(0)
        hc, hr = hx // 2, 64 * (hx % 2)
        L = f"L{si}"
        p.op("vector", lambda e, si=si: e.reciprocal(
            lrtm[:, si * 8:(si + 1) * 8], lrtm[:, si * 8:(si + 1) * 8]),
            waits=[(L, gd2)])
        p.op("vector", lambda e: e.drain())
        p.op("vector", lambda e, si=si: e.tensor_copy(
            lrtb[:, si * 8:(si + 1) * 8], lrtm[:, si * 8:(si + 1) * 8]))
        ve_rc = p.op("vector", lambda e: e.drain(), inc="VE")
        p.op("gpsimd", lambda e, si=si: e.dma_start(
            out=scr_li2[si:si + 1, :].rearrange("o (pp c) -> (o pp) c", pp=128),
            in_=lrtb[:, si * 8:(si + 1) * 8]),
            waits=[("VE", ve_rc)], inc=L, amt=16)
        gd3 = p.c(L)
        lb = ffb[:, si * 1024:(si + 1) * 1024]
        p.op("gpsimd", lambda e, si=si, lb=lb: e.dma_start(
            out=lb, in_=scr_li2[si:si + 1, :].broadcast_to([128, 1024])),
            waits=[(L, gd3), ("PO", lazy_done_ve.get(si, 0))],
            inc=L, amt=16)
        gd4 = p.c(L)
        lazy_done_ve[si] = p.op(
            "gpsimd", lambda e, lb=lb, hr=hr, hc=hc, sh=sh: e.tensor_tensor(
                xq[hr:hr + 64, hc * T + sh * 1024: hc * T + sh * 1024 + 1024],
                xq[hr:hr + 64, hc * T + sh * 1024: hc * T + sh * 1024 + 1024],
                lb[hr:hr + 64, :], ALU.mult),
            waits=[(L, gd4), ("VE", evac_ve[(hx, sh)])], inc="PO")
        if sh == 0:
            nonlocal_po[0] = p.c("PO")

    emit_builds(0, [], [("AC", ck_g_done_ac)])
    emit_builds(1, [], [("AC", ck_g_done_ac)])
    ei_count = 0
    for hp in range(NH // 2):
        h0, h1 = 2 * hp, 2 * hp + 1
        for sh in range(2):
            for t in range(TCHUNK):
                for hx in (h0, h1):
                    par = hx % 2
                    AV = AVs[par]
                    SSt = SSs[g_iter % 2]
                    Pp = Pps[g_iter % 4]
                    flush_to(3)
                    w = [("AC", ac_exp.get(g_iter - 2, 0))]
                    if g_iter < 2:
                        w.append(("AC", ck_g_done_ac))
                    if t == 0 and sh == 0:
                        w += [("VE", build_ve[hx]), (f"NG{par}", build_gd[hx])]
                    if g_iter < 2:
                        w += [("AC", ck_g_done_ac), ("VE", ck_g_done_ve)]
                    for j in range(2):
                        p.op("tensor",
                             lambda e, par=par, sh=sh, t=t, j=j, SSt=SSt: e.matmul(
                                 SSt[:, j * 512:(j + 1) * 512],
                                 kaug[par][0:96, t * 128:(t + 1) * 128],
                                 qaug[par][0:96,
                                           sh * 1024 + j * 512: sh * 1024 + (j + 1) * 512],
                                 start=True, stop=True),
                             waits=w if j == 0 else [],
                             inc="PE" if j == 1 else None)
                    pe_qk = p.c("PE")
                    if sh == 1 and t == TCHUNK - 1:
                        qk_pe_after_head[hx] = pe_qk
                    ac_exp[g_iter] = p.op(
                        "scalar", lambda e, SSt=SSt, Pp=Pp: e.activation(
                            Pp, SSt, AF.Exp, scale=8.0),
                        waits=[("PE", pe_qk), ("PE", pe_av.get(g_iter - 4, 0))],
                        inc="AC")
                    w_av = [("AC", ac_exp[g_iter])]
                    if t == 0:
                        w_av.append(("VE", evac_ac.get((hx, sh - 1), 0)))
                        if sh == 0:
                            w_av.append(("VE", evac_ac.get((hx - 2, 1), 0)))
                    pending_av.append((g_iter, hx, t, AV, Pp, w_av))
                    g_iter += 1
            flush_to(0)
            if sh == 1 and hp < NH // 2 - 1:
                emit_builds(h0 + 2, [("PE", qk_pe_after_head[h0])])
                emit_builds(h1 + 2, [("PE", qk_pe_after_head[h1])])
            for hx in (h0, h1):
                par = hx % 2
                AV = AVs[par]
                hc, hr = hx // 2, 64 * (hx % 2)
                pe_last = pe_av[g_iter - 1 - (1 - (hx - h0))]
                si = ei_count % 4
                s2 = ei_count % 2
                p.op("vector", lambda e, AV=AV, s2=s2: e.tensor_copy(
                    strips[s2], AV[0:65, :]),
                    waits=[("PE", pe_last)] + strip_users[s2] +
                          ([("VE", ck_g_done_ve)] if g_iter <= 40 else []))
                evac_ac[(hx, sh)] = p.op("vector", lambda e: e.drain(),
                                         inc="VE")
                p.op("vector", lambda e, s2=s2, hr=hr, hc=hc, sh=sh: e.tensor_copy(
                    xq[hr:hr + 64,
                       hc * T + sh * 1024: hc * T + sh * 1024 + 1024],
                    strips[s2][0:64, :]))
                evac_ve[(hx, sh)] = p.op("vector", lambda e: e.drain(),
                                         inc="VE")
                if sh == 0:
                    ve_sh0 = evac_ve[(hx, sh)]
                emit_lazy_front(ei_count, hx, sh)
                strip_users[s2] = [("VE", evac_ve[(hx, sh)]),
                                   (f"L{si}", lazy_gd1[si])]
                ei_count += 1
                if len(lazyq) > 2:
                    emit_lazy_back()
            if hp == NH // 2 - 1 and sh == 0:
                while lazyq:
                    emit_lazy_back()
        if hp == 3:
            for wc in range(NCHUNK):
                ck_wo = p.op("gpsimd", lambda e, wc=wc: e.dma_start(
                    out=qq[:, wc * 1024:(wc + 1) * 1024],
                    in_=wo[wc * 128:(wc + 1) * 128, :]),
                    waits=[("PE", qk_pe_after_head[7]), ("VE", build_ve[7])]
                    if wc == 0 else [], inc="WO", amt=16)
    ck_attn_ac = p.c("AC")
    tail = []
    while lazyq:
        ei, hx, sh, si, gd2 = lazyq.pop(0)
        p.op("vector", lambda e, si=si: e.reciprocal(
            lrtm[:, si * 8:(si + 1) * 8], lrtm[:, si * 8:(si + 1) * 8]),
            waits=[(f"L{si}", gd2)])
        p.op("vector", lambda e: e.drain())
        p.op("vector", lambda e, si=si: e.tensor_copy(
            lrtb[:, si * 8:(si + 1) * 8], lrtm[:, si * 8:(si + 1) * 8]))
        ve_rc = p.op("vector", lambda e: e.drain(), inc="VE")
        tail.append((ei, hx, sh, si, ve_rc))
    tail2 = []
    for ei, hx, sh, si, ve_rc in tail:
        p.op("gpsimd", lambda e, si=si: e.dma_start(
            out=scr_li2[si:si + 1, :].rearrange("o (pp c) -> (o pp) c", pp=128),
            in_=lrtb[:, si * 8:(si + 1) * 8]),
            waits=[("VE", ve_rc)], inc=f"L{si}", amt=16)
        gd3 = p.c(f"L{si}")
        tail2.append((ei, hx, sh, si, gd3))
    tail3 = []
    for ei, hx, sh, si, gd3 in tail2:
        lb = ffb[:, si * 1024:(si + 1) * 1024]
        p.op("gpsimd", lambda e, si=si, lb=lb: e.dma_start(
            out=lb, in_=scr_li2[si:si + 1, :].broadcast_to([128, 1024])),
            waits=[(f"L{si}", gd3), ("PO", lazy_done_ve.get(si, 0))],
            inc=f"L{si}", amt=16)
        tail3.append((ei, hx, sh, si, p.c(f"L{si}")))
    for ei, hx, sh, si, gd4 in tail3:
        hc, hr = hx // 2, 64 * (hx % 2)
        lb = ffb[:, si * 1024:(si + 1) * 1024]
        lazy_done_ve[si] = p.op(
            "gpsimd", lambda e, lb=lb, hr=hr, hc=hc, sh=sh: e.tensor_tensor(
                xq[hr:hr + 64, hc * T + sh * 1024: hc * T + sh * 1024 + 1024],
                xq[hr:hr + 64, hc * T + sh * 1024: hc * T + sh * 1024 + 1024],
                lb[hr:hr + 64, :], ALU.mult),
            waits=[(f"L{si}", gd4), ("VE", evac_ve[(hx, sh)])], inc="PO")
        if sh == 0:
            nonlocal_po[0] = p.c("PO")
    p.op("vector", lambda e: e.drain(), inc="VE")
    ck_attn_ve = p.c("VE")
    ck_attn_po = p.c("PO")

    po_sh0 = nonlocal_po[0]
    x_in = [kf[:, 0:2048], kf[:, 2048:4096]]
    out_sb = [kf[:, 4096:6144], kf[:, 6144:8192]]
    wo_view = lambda kc, oc: qq[:, kc * 1024 + oc * 128: kc * 1024 + oc * 128 + 128]
    ck_comb = {}
    ck_mult = {}
    ck_xin = {}
    pe_wo = {}

    def p7_mm_block(oc, js):
        PP = PSA if oc % 2 == 0 else PSB
        for j in js:
            for kc in range(NCHUNK):
                w2 = []
                if kc == 0:
                    if oc == 0 and j == 0:
                        w2 = [("WO", ck_wo), ("PO", po_sh0), ("VE", ve_sh0),
                              ("AC", ck_attn_ac)]
                    if oc == 1 and j == 0:
                        w2.append(("VE", evac_ac[(NH - 1, 1)]))
                    if oc == 0 and j == 2:
                        w2 += [("PO", ck_attn_po), ("VE", ck_attn_ve)]
                    if oc >= 2 and j == 0:
                        w2.append(("VE", ck_mult[oc - 2]))
                p.op("tensor", lambda e, oc=oc, kc=kc, j=j, PP=PP: e.matmul(
                    PP[:, j * 512:(j + 1) * 512], wo_view(kc, oc),
                    xq[:, kc * T + j * 512: kc * T + (j + 1) * 512],
                    start=(kc == 0), stop=(kc == NCHUNK - 1)),
                    waits=w2,
                    inc="PE" if (kc == NCHUNK - 1 and j == js[-1]) else None)
        pe_wo[oc] = p.c("PE")

    for ocp in range(0, NCHUNK, 2):
        for oc in (ocp, ocp + 1):
            wx = [("VE", build_ve[NH - 1])]
            if oc >= 2:
                wx.append(("VE", ck_comb[oc - 2]))
            ck_xin[oc] = p.op("sync", lambda e, oc=oc: e.dma_start(
                out=x_in[oc % 2], in_=xT[oc * 128:(oc + 1) * 128, :]),
                waits=wx, inc="XA" if oc % 2 == 0 else "XB", amt=16)
        p7_mm_block(ocp, (0, 1))
        p7_mm_block(ocp + 1, (0, 1))
        p7_mm_block(ocp, (2, 3))
        p7_mm_block(ocp + 1, (2, 3))
        for oc in (ocp, ocp + 1):
            PP = PSA if oc % 2 == 0 else PSB
            p.op("vector", lambda e, oc=oc, PP=PP: e.tensor_tensor(
                out_sb[oc % 2], PP[:], gg[:, oc * T:(oc + 1) * T], ALU.mult),
                waits=[("PE", pe_wo[oc])] +
                      ([("ST", oc * 16)] if oc >= 2 else []))
            ck_mult[oc] = p.op("vector", lambda e: e.drain(), inc="VE")
            ck_comb[oc] = p.op("vector", lambda e, oc=oc: e.tensor_tensor(
                out_sb[oc % 2], out_sb[oc % 2], x_in[oc % 2], ALU.add),
                waits=[("XA" if oc % 2 == 0 else "XB", ck_xin[oc])],
                inc="VE")
            p.op("sync", lambda e, oc=oc: e.dma_start(
                out=outT[oc * 128:(oc + 1) * 128, :], in_=out_sb[oc % 2]),
                waits=[("VE", ck_comb[oc])], inc="ST", amt=16)
    p.op("sync", lambda e: e.wait_ge(sems["ST"], NCHUNK * 16))

    with nc.Block() as block:
        def replay(eng_name):
            ops = p.s[eng_name]

            def body(eng):
                last = {}
                for fn, waits, inc, amt in ops:
                    for s, v in waits:
                        if last.get(s, 0) < v:
                            eng.wait_ge(sems[s], v)
                            last[s] = v
                    inst = fn(eng)
                    if inc is not None and inst is not None:
                        inst.then_inc(sems[inc], amt)
            return body

        block.sync(replay("sync"))
        block.gpsimd(replay("gpsimd"))
        block.tensor(replay("tensor"))
        block.scalar(replay("scalar"))
        block.vector(replay("vector"))

    es.close()
    return nc


def _host_constants():
    base = 10000.0
    inv_freq = 1.0 / base ** (np.arange(0, HD, 2, dtype=np.float64) / HD)
    t = np.arange(T, dtype=np.float64)
    freqs = np.outer(t, inv_freq)
    emb = np.concatenate([freqs, freqs], axis=-1)
    cos = np.cos(emb).astype(np.float32).T
    sin = np.sin(emb).astype(np.float32).T
    cos2 = np.ascontiguousarray(np.concatenate([cos, cos], axis=0))
    sin2 = np.ascontiguousarray(np.concatenate([sin, sin], axis=0))
    R = np.zeros((HD, HD), np.float32)
    for i in range(32):
        R[i, 2 * i + 1] = -1.0
    for i in range(32, 64):
        R[i, 2 * (i - 32)] = 1.0
    RT = R.T
    rt2 = np.zeros((128, 128), np.float32)
    rt2[0:64, 0:64] = RT
    rt2[64:128, 64:128] = RT
    ident = np.eye(128, dtype=np.float32)
    ehm = np.zeros((D, NH), np.float32)
    for d in range(D):
        ehm[d, d // HD] = 1.0
    return cos2, sin2, rt2, ident, ehm


def kernel(**inputs):
    f32 = lambda a: np.ascontiguousarray(np.asarray(a, dtype=np.float32))
    x = f32(inputs["L0_attn_hidden_states"])
    wq = f32(inputs["L0_attn_Wq"]).T.copy()
    wk = f32(inputs["L0_attn_Wk"]).T.copy()
    wv = f32(inputs["L0_attn_Wv"]).T.copy()
    wo = f32(inputs["L0_attn_Wo"]).T.copy()
    wff = f32(inputs["L0_ffn_W"]).T.copy()
    wp = f32(inputs["L0_ffn_Wp"]).T.copy()
    bp = f32(inputs["L0_ffn_bp"]).reshape(1, 256)
    simT = f32(inputs["L0_ffn_sim"]).T.copy()
    temp = f32(inputs["L0_ffn_temp"]).reshape(1, 1)
    wd_cat = np.concatenate([f32(inputs[f"L0_ffn_Wd{i}"]) for i in range(4)],
                            axis=0)
    wd = wd_cat.T.copy()
    wu_cat = np.concatenate([f32(inputs[f"L0_ffn_Wu{i}"]) for i in range(4)],
                            axis=1)
    wu = wu_cat.T.copy()
    cos2, sin2, rt2, ident, ehm = _host_constants()

    shared = {"wq": wq, "wk": wk, "wv": wv, "wo": wo, "wff": wff, "wp": wp,
              "wd": wd, "wu": wu, "bp": bp, "simT": simT, "temp": temp,
              "cos2": cos2, "sin2": sin2, "rt2": rt2, "ident": ident,
              "eh": ehm}
    in_maps = []
    for b in range(8):
        m = dict(shared)
        m["xT"] = np.ascontiguousarray(x[b].T)
        in_maps.append(m)

    nc = build_nc()
    res = run_bass_kernel_spmd(nc, in_maps, core_ids=list(range(8)))
    out = np.stack([np.asarray(r["outT"], np.float32).T for r in res.results],
                   axis=0)
    return np.ascontiguousarray(out)


if __name__ == "__main__":
    import reference as ref
    inputs = ref.setup_inputs()
    got = kernel(**{k: np.asarray(v) for k, v in inputs.items()})
    want = np.asarray(ref.reference(**inputs))
    num = float(np.mean((got - want) ** 2))
    den = float(np.mean(want ** 2))
    print("resvar:", num / den)


# version 38
# speedup vs baseline: 1.0243x; 1.0243x over previous
import numpy as np
from contextlib import ExitStack

import concourse.bass as bass
import concourse.mybir as mybir
from concourse.bass_utils import run_bass_kernel_spmd

F32 = mybir.dt.float32
BF16 = mybir.dt.bfloat16
AF = mybir.ActivationFunctionType
ALU = mybir.AluOpType
AX = mybir.AxisListType

D = 1024
T = 2048
NH = 16
HD = 64
NCHUNK = 8
TCHUNK = 16
CLAMP_MAX = float(np.log(1.0 / 0.01))
C_MARGIN = 5.0
C_SIGMA = 3.9

ENGINES = ("sync", "gpsimd", "tensor", "scalar", "vector")


class Plan:
    def __init__(self):
        self.s = {e: [] for e in ENGINES}
        self.n = {}

    def op(self, eng, fn, waits=(), inc=None, amt=1):
        ws = tuple((s, v) for (s, v) in waits if v > 0)
        self.s[eng].append((fn, ws, inc, amt))
        if inc is not None:
            self.n[inc] = self.n.get(inc, 0) + amt
        return self.n.get(inc, 0) if inc is not None else 0

    def c(self, sem):
        return self.n.get(sem, 0)


def build_nc():
    nc = bass.Bass("TRN2", target_bir_lowering=False)

    xT = nc.dram_tensor("xT", [D, T], F32, kind="ExternalInput").ap()
    wq = nc.dram_tensor("wq", [D, D], F32, kind="ExternalInput").ap()
    wk = nc.dram_tensor("wk", [D, D], F32, kind="ExternalInput").ap()
    wv = nc.dram_tensor("wv", [D, D], F32, kind="ExternalInput").ap()
    wo = nc.dram_tensor("wo", [D, D], F32, kind="ExternalInput").ap()
    wff = nc.dram_tensor("wff", [D, D], F32, kind="ExternalInput").ap()
    wp = nc.dram_tensor("wp", [D, 256], F32, kind="ExternalInput").ap()
    wd = nc.dram_tensor("wd", [D, 15], F32, kind="ExternalInput").ap()
    wu = nc.dram_tensor("wu", [15, D], F32, kind="ExternalInput").ap()
    bp = nc.dram_tensor("bp", [1, 256], F32, kind="ExternalInput").ap()
    simT = nc.dram_tensor("simT", [4, 256], F32, kind="ExternalInput").ap()
    temp = nc.dram_tensor("temp", [1, 1], F32, kind="ExternalInput").ap()
    cos2 = nc.dram_tensor("cos2", [128, T], F32, kind="ExternalInput").ap()
    sin2 = nc.dram_tensor("sin2", [128, T], F32, kind="ExternalInput").ap()
    rt2 = nc.dram_tensor("rt2", [128, 128], F32, kind="ExternalInput").ap()
    ident = nc.dram_tensor("ident", [128, 128], F32, kind="ExternalInput").ap()
    eh = nc.dram_tensor("eh", [D, NH], F32, kind="ExternalInput").ap()
    outT = nc.dram_tensor("outT", [D, T], F32, kind="ExternalOutput").ap()

    scr_a = nc.dram_tensor("scr_a", [1, T], F32, kind="Internal").ap()
    scr_b = nc.dram_tensor("scr_b", [1, T], F32, kind="Internal").ap()
    scr_neg = nc.dram_tensor("scr_neg", [NH, T], BF16, kind="Internal").ap()
    scr_lin = nc.dram_tensor("scr_lin", [4, 1024], F32, kind="Internal").ap()
    scr_li2 = nc.dram_tensor("scr_li2", [4, 1024], BF16, kind="Internal").ap()

    es = ExitStack()
    S = lambda name, shape, dt: es.enter_context(nc.sbuf_tensor(name, shape, dt))
    xq = S("xq", [128, NCHUNK * T], BF16)
    qq = S("qq", [128, NCHUNK * T], BF16)
    kk = S("kk", [128, NCHUNK * T], BF16)
    vv = S("vv", [128, TCHUNK * 1071], BF16)
    gg = S("gg", [128, NCHUNK * T], BF16)
    wbf = S("wbf", [128, 8192], BF16)
    ff = S("ff", [128, T], F32)
    ggs = S("ggs", [128, 6144], BF16)
    hh = S("hh", [128, 4096], BF16)
    wd_bf = S("wd_bf", [128, NCHUNK * 15], BF16)
    identb = S("identb", [128, 128], BF16)
    rt2b = S("rt2b", [128, 128], BF16)
    ehb = S("ehb", [128, NCHUNK * NH], BF16)
    simtb = S("simtb", [4, 256], BF16)
    simn_c = S("simn_c", [128, 8], BF16)
    ones_col = S("ones_col", [128, 1], BF16)
    bp_col = S("bp_col", [128, 2], F32)
    r_tm = S("r_tm", [128, TCHUNK], F32)
    rmul_tm = S("rmul_tm", [128, TCHUNK], F32)
    temp_sb = S("temp_sb", [1, 1], F32)
    esc_sb = S("esc_sb", [1, 1], F32)
    eps1 = S("eps1", [1, 1], F32)
    epst = S("epst", [1, 1], F32)
    lrtm = S("lrtm", [128, 32], F32)
    lrtb = S("lrtb", [128, 32], BF16)
    rms16 = S("rms16", [16, 1], F32)
    rms16b = S("rms16b", [16, 1], F32)
    simnn = S("simnn", [4, 1], F32)

    PSA = es.enter_context(nc.psum_tensor("PSA", [128, T], F32))
    PSB = es.enter_context(nc.psum_tensor("PSB", [128, T], F32))

    sems = {}
    for sname in ("WD", "PE", "AC", "VE", "ST", "PO", "BN", "CN",
                  "WB", "WK", "WV", "WO", "WU", "WF", "WG", "NG0", "NG1",
                  "ST0", "ST1", "ST2", "ST3", "XA", "XB",
                  "X0", "X1", "X2", "X3", "L0", "L1", "L2", "L3"):
        sems[sname] = es.enter_context(nc.semaphore(sname))

    p = Plan()
    hf = hh[:].bitcast(F32)
    kf = kk[:].bitcast(F32)
    lt = hf[:, 0:64]
    p4g = hf[:, 64:128]
    mask = hf[:, 128:192]
    w4 = hf[:, 192:256]
    max_e = hf[:, 256:272]
    nm_e = hf[:, 272:288]
    se_e = hf[:, 288:304]
    rse_e = hf[:, 304:320]
    dtm = hh[:, 640:896]
    dwtm = hh[:, 896:1152]
    wu_bf = ggs[0:15, 0:1024]
    xc_chunk = lambda c: xq[:, c * T:(c + 1) * T]

    for c in range(NCHUNK):
        p.op("gpsimd", lambda e, c=c: e.dma_start(
            out=xq[:, c * T:(c + 1) * T],
            in_=xT[c * 128:(c + 1) * 128, :]),
            inc=f"X{c // 2}", amt=16)
    xw = lambda c: (f"X{c // 2}", 32)
    ck_wq = p.op("gpsimd", lambda e: e.dma_start(
        out=wbf[:].rearrange("p (c n) -> p c n", c=NCHUNK),
        in_=wq.rearrange("(c p) n -> p c n", p=128)), inc="WB", amt=16)
    p.op("gpsimd", lambda e: e.dma_start(out=identb[:], in_=ident),
         inc="CN", amt=16)
    p.op("gpsimd", lambda e: e.dma_start(out=rt2b[:], in_=rt2),
         inc="CN", amt=16)
    p.op("gpsimd", lambda e: e.dma_start(
        out=ehb[:].rearrange("p (c n) -> p c n", c=NCHUNK),
        in_=eh.rearrange("(c p) n -> p c n", p=128)), inc="CN", amt=16)
    p.op("gpsimd", lambda e: e.dma_start(out=simtb[:], in_=simT),
         inc="CN", amt=16)
    ck_cn = p.op("gpsimd", lambda e: e.dma_start(
        out=wd_bf[:].rearrange("p (c n) -> p c n", c=NCHUNK),
        in_=wd.rearrange("(c p) n -> p c n", p=128)), inc="CN", amt=16)
    ck_ident = ck_rt2 = ck_eh = ck_sim = ck_wd = ck_cn
    for oc2 in range(2):
        ck_bp = p.op("sync", lambda e, oc2=oc2: e.dma_start(
            out=bp_col[:, oc2:oc2 + 1],
            in_=bp[0:1, oc2 * 128:(oc2 + 1) * 128].rearrange(
                "o (pp c) -> (o pp) c", pp=128)),
            inc="WD", amt=16)
    ck_temp = p.op("sync", lambda e: e.dma_start(out=temp_sb[:], in_=temp),
                   inc="WD", amt=16)
    for tci in range(TCHUNK):
        p.op("vector", lambda e, tci=tci: e.memset(
            vv[:, tci * 1071: tci * 1071 + 1040].rearrange(
                "p (h d) -> p h d", d=65)[:, :, 64:65], 1.0))
        p.op("vector", lambda e, tci=tci: e.memset(
            vv[:, tci * 1071 + 1040: (tci + 1) * 1071], 0.0))
    p.op("vector", lambda e: e.memset(ones_col[:], 1.0))
    p.op("vector", lambda e: e.memset(eps1[:], 1e-5))
    ck_ones = p.op("vector", lambda e: e.memset(epst[:], 1e-24), inc="VE")

    sqv = ggs[:, 0:2048]
    SUMps = PSA[0:1, :]
    SSQps = PSB[0:1, :]
    pe_prev = 0
    for c in range(NCHUNK):
        ve_sq = p.op("vector", lambda e, c=c: e.tensor_tensor(
            sqv, xc_chunk(c), xc_chunk(c), ALU.mult),
            waits=[xw(c), ("PE", pe_prev),
                   ("VE", ck_ones if c == 0 else 0)], inc="VE")
        for j in range(4):
            p.op("tensor", lambda e, c=c, j=j: e.matmul(
                SUMps[:, j * 512:(j + 1) * 512], ones_col[:],
                xc_chunk(c)[:, j * 512:(j + 1) * 512],
                start=(c == 0), stop=(c == NCHUNK - 1), skip_group_check=True),
                waits=[xw(c), ("VE", ck_ones)]
                if j == 0 else [])
        for j in range(4):
            p.op("tensor", lambda e, c=c, j=j: e.matmul(
                SSQps[:, j * 512:(j + 1) * 512], ones_col[:],
                sqv[:, j * 512:(j + 1) * 512],
                start=(c == 0), stop=(c == NCHUNK - 1), skip_group_check=True),
                waits=[("VE", ve_sq)] if j == 0 else [],
                inc="PE" if j == 3 else None)
        pe_prev = p.c("PE")
    ck_stats_pe = p.c("PE")
    mean_r = hf[0:1, :]
    e2_r = kf[0:1, 0:2048]
    msq_r = kf[0:1, 2048:4096]
    ck_mean = p.op("scalar", lambda e: e.mul(mean_r, SUMps, 1.0 / D),
                   waits=[("PE", ck_stats_pe)], inc="AC")
    ck_e2 = p.op("scalar", lambda e: e.mul(e2_r, SSQps, 1.0 / D), inc="AC")
    mean_bc = ggs[:, 4096:6144]
    p.op("gpsimd", lambda e: e.dma_start(out=scr_a, in_=mean_r),
         waits=[("AC", ck_mean), ("BN", p.c("BN"))], inc="BN", amt=16)
    ck_meanbc = p.op("gpsimd", lambda e: e.dma_start(
        out=mean_bc, in_=scr_a.broadcast_to([128, T])),
        waits=[("BN", p.c("BN"))], inc="BN", amt=16)
    p.op("vector", lambda e: e.tensor_tensor(
        msq_r, mean_r, mean_r, ALU.mult), waits=[("AC", ck_mean)])
    p.op("vector", lambda e: e.drain())
    ck_var = p.op("vector", lambda e: e.tensor_tensor(
        msq_r, e2_r, msq_r, ALU.subtract), waits=[("AC", ck_e2)], inc="VE")
    ck_sroot = p.op("scalar", lambda e: e.activation(
        e2_r, msq_r, AF.Ln, bias=eps1[:]),
        waits=[("VE", ck_var), ("VE", ck_ones)], inc="AC")
    p.op("scalar", lambda e: e.drain())
    r_row = hf[0:1, :]
    p.op("scalar", lambda e: e.activation(
        r_row, e2_r, AF.Exp, scale=-0.5), waits=[("BN", ck_meanbc)])
    ck_rrow = p.op("scalar", lambda e: e.drain(), inc="AC")
    p.op("gpsimd", lambda e: e.dma_start(out=scr_b, in_=r_row),
         waits=[("AC", ck_rrow), ("BN", p.c("BN"))], inc="BN", amt=16)
    gd_rb = p.c("BN")
    ck_rbc = p.op("gpsimd", lambda e: e.dma_start(
        out=ff[:], in_=scr_b.broadcast_to([128, T])),
        waits=[("BN", gd_rb)], inc="BN", amt=16)
    for tc2 in range(TCHUNK):
        ck_rtm = p.op("sync", lambda e, tc2=tc2: e.dma_start(
            out=r_tm[:, tc2:tc2 + 1],
            in_=scr_b[0:1, tc2 * 128:(tc2 + 1) * 128].rearrange(
                "o (pp c) -> (o pp) c", pp=128)),
            waits=[("BN", gd_rb)] if tc2 == 0 else [], inc="WD", amt=16)
    ck_bp = ck_temp = ck_rtm

    ck_xc = []
    for c in range(NCHUNK):
        ck_xc.append(p.op("vector", lambda e, c=c: e.tensor_tensor(
            xc_chunk(c), xc_chunk(c), mean_bc, ALU.subtract),
            waits=[("BN", ck_meanbc)] + ([("PE", ck_stats_pe)] if c == 0 else []),
            inc="VE"))
    ck_xc_all = p.c("VE")

    cosr = ggs[:, 0:2048]
    sinr = ggs[:, 2048:4096]
    ck_ld_cos = p.op("gpsimd", lambda e: e.dma_start(out=mean_bc, in_=cos2),
                     waits=[("VE", ck_xc_all), ("PE", ck_stats_pe),
                            ("BN", p.c("BN"))], inc="BN", amt=16)
    ck_cosr = p.op("vector", lambda e: e.tensor_tensor(
        cosr, mean_bc, ff[:], ALU.mult),
        waits=[("BN", ck_ld_cos), ("BN", ck_rbc), ("PE", ck_stats_pe)],
        inc="VE")
    ck_ld_sin = p.op("gpsimd", lambda e: e.dma_start(out=mean_bc, in_=sin2),
                     waits=[("VE", ck_cosr), ("BN", p.c("BN"))],
                     inc="BN", amt=16)
    ck_sinr = p.op("vector", lambda e: e.tensor_tensor(
        sinr, mean_bc, ff[:], ALU.mult), waits=[("BN", ck_ld_sin)], inc="VE")
    ck_wk = p.op("gpsimd", lambda e: e.dma_start(
        out=gg[:, 0:8192].rearrange("p (c n) -> p c n", c=NCHUNK),
        in_=wk.rearrange("(c p) n -> p c n", p=128)), inc="WK", amt=16)

    def proj_fm(wview, dest_fn, extra_wait_first=(), kc_waits=None):
        evac_cks = []
        for oc in range(NCHUNK):
            PP = PSA if oc % 2 == 0 else PSB
            for kc in range(NCHUNK):
                for j in range(4):
                    w = []
                    if kc == 0 and j == 0:
                        if oc == 0:
                            w = list(extra_wait_first)
                        if oc >= 2:
                            w.append(("AC", evac_cks[oc - 2]))
                    if oc == 0 and j == 0 and kc_waits:
                        w = w + kc_waits.get(kc, [])
                    is_last = (kc == NCHUNK - 1 and j == 3)
                    p.op("tensor", lambda e, oc=oc, kc=kc, j=j, PP=PP: e.matmul(
                        PP[:, j * 512:(j + 1) * 512], wview(kc, oc),
                        xc_chunk(kc)[:, j * 512:(j + 1) * 512],
                        start=(kc == 0), stop=(kc == NCHUNK - 1)),
                        waits=w, inc="PE" if is_last else None)
            last_pe = p.c("PE")
            evac_cks.append(p.op("scalar", lambda e, oc=oc, PP=PP: e.copy(
                dest_fn(oc), PP[:]), waits=[("PE", last_pe)], inc="AC"))
        return p.c("PE"), evac_cks

    wq_view = lambda kc, oc: wbf[:, kc * 1024 + oc * 128: kc * 1024 + oc * 128 + 128]
    ck_qpe, ck_qev = proj_fm(
        wq_view, lambda oc: qq[:, oc * T:(oc + 1) * T],
        extra_wait_first=[("WB", ck_wq), ("AC", ck_sroot)],
        kc_waits={c: [("VE", ck_xc[c])] for c in range(NCHUNK)})
    ck_wv = p.op("gpsimd", lambda e: e.dma_start(
        out=gg[:, 8192:16384].rearrange("p (c n) -> p c n", c=NCHUNK),
        in_=wv.rearrange("(c p) n -> p c n", p=128)), inc="WV", amt=16)
    ck_wp = p.op("gpsimd", lambda e: e.dma_start(
        out=wbf[:, 0:2048].rearrange("p (c n) -> p c n", c=NCHUNK),
        in_=wp.rearrange("(c p) n -> p c n", p=128)),
        waits=[("PE", ck_qpe)], inc="WB", amt=16)
    ck_wffa = p.op("gpsimd", lambda e: e.dma_start(
        out=wbf[:, 2048:8192].rearrange("p (c n) -> p c n", c=6),
        in_=wff.rearrange("(c p) n -> p c n", p=128)[:, 2:8, :]),
        inc="WF", amt=16)
    wk_view = lambda kc, oc: gg[:, kc * 1024 + oc * 128: kc * 1024 + oc * 128 + 128]

    rqbA = ggs[:, 4096:5120]
    t1A = ggs[:, 5120:6144]
    t2A = hh[:, 0:1024]
    rqbB = hh[:, 1024:2048]
    t1B = hh[:, 2048:3072]
    t2B = hh[:, 3072:4096]
    rope_ve = {}
    RQh = [PSA[:, 0:1024], PSA[:, 1024:2048]]
    vslots = [PSB[:, 0:1024], PSB[:, 1024:2048]]
    ck_vev = []
    ck_kev_pc = []
    rope_plan = []
    for src, evs in ((qq, ck_qev), (kk, ck_kev_pc)):
        for c in range(NCHUNK):
            for h2 in range(2):
                rope_plan.append((src, evs, c, h2))

    def emit_rope_pair(ih):
        sA, eA, cA, hA = rope_plan[ih]
        sB, eB, cB, hB = rope_plan[ih + 1]
        slA = slice(cA * T + hA * 1024, cA * T + (hA + 1) * 1024)
        slB = slice(cB * T + hB * 1024, cB * T + (hB + 1) * 1024)
        psA = slice(hA * 1024, (hA + 1) * 1024)
        psB = slice(hB * 1024, (hB + 1) * 1024)
        for k, (src, evs, c, RQ, sl) in enumerate(
                ((sA, eA, cA, RQh[0], slA), (sB, eB, cB, RQh[1], slB))):
            w0 = [("AC", evs[c])]
            if ih == 0 and k == 0:
                w0 += [("AC", ck_qev[7]), ("CN", ck_rt2)]
            if ih >= 2:
                w0.append(("VE", rope_ve[ih - 2]))
            for j in range(2):
                p.op("tensor", lambda e, src=src, sl=sl, j=j, RQ=RQ: e.matmul(
                    RQ[:, j * 512:(j + 1) * 512], rt2b[:],
                    src[:, sl.start + j * 512: sl.start + (j + 1) * 512],
                    start=True, stop=True),
                    waits=w0 if j == 0 else [], inc="PE" if j == 1 else None)
        pe_rq = p.c("PE")
        p.op("vector", lambda e: e.tensor_copy(rqbA, RQh[0][:]),
             waits=[("PE", pe_rq)] + ([("VE", ck_sinr)] if ih == 0 else []))
        p.op("vector", lambda e: e.tensor_copy(rqbB, RQh[1][:]))
        p.op("vector", lambda e, sA=sA, slA=slA, psA=psA: e.tensor_tensor(
            t1A, sA[:, slA], cosr[:, psA], ALU.mult))
        p.op("vector", lambda e, sB=sB, slB=slB, psB=psB: e.tensor_tensor(
            t1B, sB[:, slB], cosr[:, psB], ALU.mult))
        p.op("vector", lambda e: e.drain())
        p.op("vector", lambda e, psA=psA: e.tensor_tensor(
            t2A, rqbA, sinr[:, psA], ALU.mult))
        p.op("vector", lambda e, psB=psB: e.tensor_tensor(
            t2B, rqbB, sinr[:, psB], ALU.mult))
        p.op("vector", lambda e: e.drain())
        p.op("vector", lambda e, sA=sA, slA=slA: e.tensor_tensor(
            sA[:, slA], t1A, t2A, ALU.add))
        rope_ve[ih] = rope_ve[ih + 1] = p.op(
            "vector", lambda e, sB=sB, slB=slB: e.tensor_tensor(
                sB[:, slB], t1B, t2B, ALU.add), inc="VE")

    def emit_v_chunk(tc):
        PV = vslots[tc % 2]
        for kc in range(NCHUNK):
            for j in range(2):
                w = []
                if kc == 0 and j == 0:
                    if tc == 0:
                        w = [("WV", ck_wv), ("AC", kev_h[-1])]
                    if tc >= 2:
                        w.append(("AC", ck_vev[tc - 2]))
                p.op("tensor", lambda e, tc=tc, kc=kc, j=j, PV=PV: e.matmul(
                    PV[:, j * 512:(j + 1) * 512],
                    xc_chunk(kc)[:, tc * 128:(tc + 1) * 128],
                    gg[:, 8192 + kc * 1024 + j * 512: 8192 + kc * 1024 + (j + 1) * 512],
                    start=(kc == 0), stop=(kc == NCHUNK - 1)),
                    waits=w,
                    inc="PE" if (kc == NCHUNK - 1 and j == 1) else None)
        pe_v = p.c("PE")
        ck_vev.append(p.op("scalar", lambda e, tc=tc, PV=PV: e.mul(
            vv[:, tc * 1071: tc * 1071 + 1040].rearrange(
                "p (h d) -> p h d", d=65)[:, :, 0:64],
            PV[:].rearrange("p (h d) -> p h d", d=64),
            r_tm[:, tc:tc + 1]),
            waits=[("PE", pe_v), ("WD", ck_rtm)], inc="AC"))

    kev_h = []
    for oc in range(NCHUNK):
        for h in range(2):
            hu = 2 * oc + h
            PU = PSB[:, h * 1024:(h + 1) * 1024]
            for kc in range(NCHUNK):
                for j in range(2):
                    w = []
                    if kc == 0 and j == 0:
                        if hu == 0:
                            w = [("WK", ck_wk), ("AC", ck_qev[7]),
                                 ("AC", ck_rrow)]
                        if hu >= 2:
                            w.append(("AC", kev_h[hu - 2]))
                    p.op("tensor",
                         lambda e, oc=oc, h=h, kc=kc, j=j, PU=PU: e.matmul(
                             PU[:, j * 512:(j + 1) * 512], wk_view(kc, oc),
                             xc_chunk(kc)[:, h * 1024 + j * 512:
                                          h * 1024 + (j + 1) * 512],
                             start=(kc == 0), stop=(kc == NCHUNK - 1)),
                         waits=w,
                         inc="PE" if (kc == NCHUNK - 1 and j == 1) else None)
            pe_h = p.c("PE")
            kev_h.append(p.op("scalar", lambda e, oc=oc, h=h, PU=PU: e.copy(
                kk[:, oc * T + h * 1024: oc * T + (h + 1) * 1024], PU),
                waits=[("PE", pe_h)], inc="AC"))
        ck_kev_pc.append(kev_h[-1])
        emit_rope_pair(2 * oc)
    for tc in range(TCHUNK):
        emit_v_chunk(tc)
        if tc % 2 == 0:
            emit_rope_pair(16 + tc)
    ck_rope_all = p.c("VE")
    ck_v_done = p.c("AC")

    p.op("vector", lambda e: e.drain())
    sqscrs = [ggs[:, 4096:6144], ggs[:, 0:2048]]
    SSQQ = PSA[0:16, :]
    SSK = PSB[0:16, :]
    pe_done = {}
    ci = 0
    for src, SS in ((qq, SSQQ), (kk, SSK)):
        for c in range(NCHUNK):
            sq_t = sqscrs[ci % 2]
            ve_sq = p.op("vector", lambda e, src=src, c=c, sq_t=sq_t: e.tensor_tensor(
                sq_t, src[:, c * T:(c + 1) * T], src[:, c * T:(c + 1) * T],
                ALU.mult), waits=[("PE", pe_done.get(ci - 2, 0))], inc="VE")
            for j in range(4):
                p.op("tensor", lambda e, c=c, j=j, SS=SS, sq_t=sq_t: e.matmul(
                    SS[:, j * 512:(j + 1) * 512], ehb[:, c * NH:(c + 1) * NH],
                    sq_t[:, j * 512:(j + 1) * 512],
                    start=(c == 0), stop=(c == NCHUNK - 1),
                    skip_group_check=True),
                    waits=([("VE", ve_sq)] +
                           ([("CN", ck_eh), ("AC", ck_v_done)] if (c == 0 and ci < 2) else []))
                    if j == 0 else [],
                    inc="PE" if j == 3 else None)
            pe_done[ci] = p.c("PE")
            ci += 1
    ck_cm_pe = p.c("PE")
    ck_sqq = p.op("scalar", lambda e: e.activation(
        hh[0:16, 0:2048], SSQQ, AF.Sqrt),
        waits=[("PE", pe_done[NCHUNK - 1])], inc="AC")
    ck_sks = p.op("vector", lambda e: e.tensor_reduce(
        rms16[:], SSK, axis=AX.X, op=ALU.add),
        waits=[("PE", ck_cm_pe)], inc="VE")
    ck_rms = p.op("scalar", lambda e: e.activation(
        rms16b[:], rms16[:], AF.Sqrt, scale=1.0 / (T * HD)),
        waits=[("VE", ck_sks)], inc="AC")
    negc = ggs[0:16, 2048:4096]
    p.op("vector", lambda e: e.tensor_scalar(
        negc, hh[0:16, 0:2048], rms16b[:], -C_SIGMA, ALU.mult, ALU.mult),
        waits=[("AC", ck_rms), ("AC", ck_sqq)])
    p.op("vector", lambda e: e.drain())
    p.op("vector", lambda e: e.tensor_scalar(
        negc, negc, -C_MARGIN, None, ALU.add))
    ck_negc_ve = p.op("vector", lambda e: e.drain(), inc="VE")
    ck_negc = p.op("gpsimd", lambda e: e.dma_start(out=scr_neg, in_=negc),
                   waits=[("VE", ck_negc_ve), ("BN", p.c("BN"))],
                   inc="BN", amt=16)

    wp_view = lambda kc, oc: wbf[:, kc * 256 + oc * 128: kc * 256 + oc * 128 + 128]
    pbf = [ggs[:, 0:2048], ggs[:, 2048:4096]]
    ck_pb = []
    ve_pm = []
    for oc in range(2):
        for kc in range(NCHUNK):
            for j in range(4):
                w = []
                if kc == 0 and j == 0:
                    w = [("WB", ck_wp), ("AC", ck_v_done), ("VE", ck_sks)]
                    if oc == 1:
                        w.append(("VE", ve_pm[0]))
                p.op("tensor", lambda e, oc=oc, kc=kc, j=j: e.matmul(
                    PSB[:, j * 512:(j + 1) * 512], wp_view(kc, oc),
                    xc_chunk(kc)[:, j * 512:(j + 1) * 512],
                    start=(kc == 0), stop=(kc == NCHUNK - 1)),
                    waits=w,
                    inc="PE" if (kc == NCHUNK - 1 and j == 3) else None)
        pe_p = p.c("PE")
        ve_pm.append(p.op("vector", lambda e, oc=oc: e.tensor_tensor(
            pbf[oc], PSB[:], ff[:], ALU.mult),
            waits=[("PE", pe_p), ("BN", ck_rbc)] +
                  ([("VE", ck_negc_ve)] if oc == 0 else
                   [("BN", ck_negc)]), inc="VE"))
        ck_pb.append(p.op("scalar", lambda e, oc=oc: e.add(
            pbf[oc], pbf[oc], bp_col[:, oc:oc + 1]),
            waits=[("VE", ve_pm[-1]), ("WD", ck_bp)], inc="AC"))
    ck_wffb = p.op("gpsimd", lambda e: e.dma_start(
        out=wbf[:, 0:2048].rearrange("p (c n) -> p c n", c=2),
        in_=wff.rearrange("(c p) n -> p c n", p=128)[:, 0:2, :]),
        waits=[("PE", p.c("PE"))], inc="WG", amt=16)

    wff_view = lambda kc, oc: wbf[:, kc * 1024 + oc * 128: kc * 1024 + oc * 128 + 128]
    KCORD = [2, 3, 4, 5, 6, 7, 0, 1]
    p4a_ev = {}
    p4a_pe = {}

    def p4a_mms(u):
        oc, h = u // 2, u % 2
        PU = PSA[:, h * 1024:(h + 1) * 1024]
        for ki, kc in enumerate(KCORD):
            for j in range(2):
                w = []
                if ki == 0 and j == 0:
                    if u == 0:
                        w = [("WF", ck_wffa), ("AC", ck_sqq)]
                    if u >= 2:
                        w.append(("AC", p4a_ev[u - 2]))
                if kc == 0 and j == 0 and u == 0:
                    w.append(("WG", ck_wffb))
                p.op("tensor",
                     lambda e, oc=oc, h=h, kc=kc, j=j, PU=PU, ki=ki: e.matmul(
                         PU[:, j * 512:(j + 1) * 512], wff_view(kc, oc),
                         xc_chunk(kc)[:, h * 1024 + j * 512:
                                      h * 1024 + (j + 1) * 512],
                         start=(ki == 0), stop=(ki == NCHUNK - 1)),
                     waits=w,
                     inc="PE" if (ki == NCHUNK - 1 and j == 1) else None)
        p4a_pe[u] = p.c("PE")

    def p4a_evac(u):
        oc, h = u // 2, u % 2
        PU = PSA[:, h * 1024:(h + 1) * 1024]
        p4a_ev[u] = p.op("scalar", lambda e, oc=oc, h=h, PU=PU: e.copy(
            gg[:, oc * T + h * 1024: oc * T + (h + 1) * 1024], PU),
            waits=[("PE", p4a_pe[u])], inc="AC")

    p4a_mms(0)
    p4a_mms(1)
    p4a_evac(0)
    p4a_mms(2)
    p4a_evac(1)
    p4a_mms(3)
    SSP = PSB[0:1, :]
    sq2 = ggs[:, 4096:6144]
    pe_prev = p.c("PE")
    for oc in range(2):
        ve_sq = p.op("vector", lambda e, oc=oc: e.tensor_tensor(
            sq2, pbf[oc], pbf[oc], ALU.mult),
            waits=[("AC", ck_pb[oc]), ("PE", pe_prev)], inc="VE")
        for j in range(4):
            p.op("tensor", lambda e, oc=oc, j=j: e.matmul(
                SSP[:, j * 512:(j + 1) * 512], ones_col[:],
                sq2[:, j * 512:(j + 1) * 512],
                start=(oc == 0), stop=(oc == 1), skip_group_check=True),
                waits=[("VE", ve_sq)] if j == 0 else [],
                inc="PE" if j == 3 else None)
        pe_prev = p.c("PE")
    ck_ssp_pe = p.c("PE")
    p4a_evac(2)
    p4a_mms(4)
    simsq = hf[0:4, 0:256]
    ck_simsq = p.op("scalar", lambda e: e.activation(
        simsq, simtb[:], AF.Square), waits=[("CN", ck_sim)], inc="AC")
    ck_simn1 = p.op("vector", lambda e: e.tensor_reduce(
        simnn[:], simsq, axis=AX.X, op=ALU.add),
        waits=[("AC", ck_simsq)], inc="VE")
    ck_simn2 = p.op("scalar", lambda e: e.activation(
        simnn[:], simnn[:], AF.Sqrt), waits=[("VE", ck_simn1)], inc="AC")
    p.op("vector", lambda e: e.reciprocal(simnn[:], simnn[:]),
         waits=[("AC", ck_simn2)])
    p.op("vector", lambda e: e.drain())
    ve_simn = p.op("vector", lambda e: e.tensor_scalar(
        simtb[:], simtb[:], simnn[:], None, ALU.mult), inc="VE")
    rmul_r = hf[0:1, :]
    ck_lnp = p.op("scalar", lambda e: e.activation(
        rmul_r, SSP, AF.Ln, bias=epst[:]),
        waits=[("PE", ck_ssp_pe), ("VE", ck_simn1)], inc="AC")
    p.op("scalar", lambda e: e.drain())
    ck_sqrtp = p.op("scalar", lambda e: e.activation(
        rmul_r, rmul_r, AF.Exp, scale=-0.5), inc="AC")
    p4a_evac(3)
    p.op("vector", lambda e: e.tensor_scalar(
        esc_sb[:], temp_sb[:], CLAMP_MAX, None, ALU.min),
        waits=[("WD", ck_temp)])
    ve_esc = p.op("vector", lambda e: e.drain(), inc="VE")
    ck_esc = p.op("scalar", lambda e: e.activation(
        esc_sb[:], esc_sb[:], AF.Exp), waits=[("VE", ve_esc)], inc="AC")
    p.op("vector", lambda e: e.tensor_scalar(
        rmul_r, rmul_r, esc_sb[:], None, ALU.mult),
        waits=[("AC", ck_esc), ("AC", ck_sqrtp)])
    ck_rmul = p.op("vector", lambda e: e.drain(), inc="VE")
    PSpt = PSB[:].bitcast(BF16)[:, 0:8]
    for half in range(2):
        p.op("tensor", lambda e, half=half: e.transpose(
            PSpt[:, half * 4:(half + 1) * 4],
            simtb[:, half * 128:(half + 1) * 128], identb[0:4, 0:4]),
            waits=[("VE", ve_simn), ("CN", ck_ident),
                   ("AC", ck_lnp)] if half == 0 else [],
            inc="PE" if half == 1 else None)
    ck_simn_pe = p.c("PE")
    p4a_mms(5)
    p.op("vector", lambda e: e.tensor_copy(simn_c[:], PSpt),
         waits=[("PE", ck_simn_pe)])
    ck_simn_c = p.op("vector", lambda e: e.drain(), inc="VE")
    p4a_evac(4)
    LU = PSB[0:4, :]
    for oc in range(2):
        for j in range(4):
            p.op("tensor", lambda e, oc=oc, j=j: e.matmul(
                LU[:, j * 512:(j + 1) * 512],
                simn_c[:, oc * 4:(oc + 1) * 4],
                pbf[oc][:, j * 512:(j + 1) * 512],
                start=(oc == 0), stop=(oc == 1), skip_group_check=True),
                waits=[("VE", ck_simn_c),
                       ("AC", ck_sqrtp)] if (oc == 0 and j == 0) else [],
                inc="PE" if (oc == 1 and j == 3) else None)
    ck_lu_pe = p.c("PE")
    p4a_mms(6)
    p4a_evac(5)
    ck_wu = p.op("gpsimd", lambda e: e.dma_start(out=wu_bf, in_=wu),
                 waits=[("PE", ck_lu_pe)], inc="WU", amt=16)
    p.op("gpsimd", lambda e: e.dma_start(out=scr_a, in_=rmul_r),
         waits=[("VE", ck_rmul), ("BN", p.c("BN"))], inc="BN", amt=16)
    gd_ra = p.c("BN")
    ck_rmt = 0
    for tc2 in range(TCHUNK):
        ck_rmt = p.op("sync", lambda e, tc2=tc2: e.dma_start(
            out=rmul_tm[:, tc2:tc2 + 1],
            in_=scr_a[0:1, tc2 * 128:(tc2 + 1) * 128].rearrange(
                "o (pp c) -> (o pp) c", pp=128)),
            waits=[("BN", gd_ra)] if tc2 == 0 else [], inc="WD", amt=16)
    lsc = ggs[0:4, 4096:6144]
    ck_lsc_a = p.op("scalar", lambda e: e.copy(lsc, LU),
                    waits=[("PE", ck_lu_pe)], inc="AC")
    PSlt = PSB[:].bitcast(BF16)[:, 0:64]
    for j in range(TCHUNK):
        p.op("tensor", lambda e, j=j: e.transpose(
            PSlt[:, j * 4:(j + 1) * 4], lsc[:, j * 128:(j + 1) * 128],
            identb[0:4, 0:4]),
            waits=[("AC", ck_lsc_a)] if j == 0 else [],
            inc="PE" if j == TCHUNK - 1 else None)
    ck_lt_pe = p.c("PE")
    p4a_mms(7)
    p4a_evac(6)
    p.op("vector", lambda e: e.tensor_copy(lt, PSlt),
         waits=[("PE", ck_lt_pe), ("BN", gd_ra)])
    p.op("vector", lambda e: e.drain())
    for j in range(TCHUNK):
        p.op("vector", lambda e, j=j: e.tensor_scalar(
            lt[:, j * 4:(j + 1) * 4], lt[:, j * 4:(j + 1) * 4],
            rmul_tm[:, j:j + 1], None, ALU.mult),
            waits=[("WD", ck_rmt)] if j == 0 else [])
    ck_lt = p.op("vector", lambda e: e.drain(), inc="VE")
    for j in range(TCHUNK):
        p.op("vector", lambda e, j=j: e.tensor_reduce(
            max_e[:, j:j + 1], lt[:, j * 4:(j + 1) * 4], axis=AX.X, op=ALU.max))
    p.op("vector", lambda e: e.drain())
    p.op("vector", lambda e: e.tensor_scalar(
        nm_e, max_e, -1.0, None, ALU.mult))
    ve_nm = p.op("vector", lambda e: e.drain(), inc="VE")
    p4a_mms(8)
    p4a_evac(7)
    ck_exps = 0
    for j in range(TCHUNK):
        ck_exps = p.op("scalar", lambda e, j=j: e.activation(
            p4g[:, j * 4:(j + 1) * 4], lt[:, j * 4:(j + 1) * 4], AF.Exp,
            bias=nm_e[:, j:j + 1]),
            waits=[("VE", ve_nm)] if j == 0 else [], inc="AC")
    for j in range(TCHUNK):
        p.op("vector", lambda e, j=j: e.tensor_reduce(
            se_e[:, j:j + 1], p4g[:, j * 4:(j + 1) * 4], axis=AX.X, op=ALU.add),
            waits=[("AC", ck_exps)] if j == 0 else [])
    p.op("vector", lambda e: e.drain())
    p.op("vector", lambda e: e.reciprocal(rse_e, se_e))
    p.op("vector", lambda e: e.drain())
    for j in range(TCHUNK):
        p.op("vector", lambda e, j=j: e.tensor_scalar(
            mask[:, j * 4:(j + 1) * 4], lt[:, j * 4:(j + 1) * 4],
            max_e[:, j:j + 1], None, ALU.is_ge))
    p.op("vector", lambda e: e.drain())
    p.op("vector", lambda e: e.tensor_tensor(w4, mask, p4g, ALU.mult))
    p.op("vector", lambda e: e.drain())
    for j in range(TCHUNK):
        p.op("vector", lambda e, j=j: e.tensor_scalar(
            w4[:, j * 4:(j + 1) * 4], w4[:, j * 4:(j + 1) * 4],
            rse_e[:, j:j + 1], None, ALU.mult))
    ck_w4 = p.op("vector", lambda e: e.drain(), inc="VE")
    p4a_mms(9)
    p4a_evac(8)

    DW = PSB[0:15, :]
    for kc in range(NCHUNK):
        for j in range(4):
            p.op("tensor", lambda e, kc=kc, j=j: e.matmul(
                DW[:, j * 512:(j + 1) * 512],
                wd_bf[:, kc * 15:(kc + 1) * 15],
                xc_chunk(kc)[:, j * 512:(j + 1) * 512],
                start=(kc == 0), stop=(kc == NCHUNK - 1)),
                waits=[("CN", ck_wd), ("VE", ck_lt)] if (kc == 0 and j == 0) else [],
                inc="PE" if (kc == NCHUNK - 1 and j == 3) else None)
    ck_dw_pe = p.c("PE")
    p4a_mms(10)
    down_sb = ggs[0:15, 4096:6144]
    ck_down = p.op("scalar", lambda e: e.copy(down_sb, DW),
                   waits=[("PE", ck_dw_pe)], inc="AC")
    p4a_evac(9)
    PSdt = PSB[:].bitcast(BF16)[:, 64:320]
    for j in range(TCHUNK):
        p.op("tensor", lambda e, j=j: e.transpose(
            PSdt[:, j * 16: j * 16 + 15], down_sb[:, j * 128:(j + 1) * 128],
            identb[0:15, 0:15]),
            waits=[("AC", ck_down), ("VE", ck_lt)] if j == 0 else [],
            inc="PE" if j == TCHUNK - 1 else None)
    ck_dt_pe = p.c("PE")
    p4a_mms(11)
    p.op("vector", lambda e: e.tensor_copy(dtm, PSdt),
         waits=[("PE", ck_dt_pe), ("BN", gd_ra)])
    ck_dtm = p.op("vector", lambda e: e.drain(), inc="VE")
    p4a_evac(10)
    offs = [0, 1, 3, 7, 15]
    for j in range(TCHUNK):
        for ei in range(4):
            a, b = offs[ei], offs[ei + 1]
            p.op("vector", lambda e, j=j, ei=ei, a=a, b=b: e.tensor_scalar(
                dwtm[:, j * 16 + a: j * 16 + b], dtm[:, j * 16 + a: j * 16 + b],
                w4[:, j * 4 + ei: j * 4 + ei + 1], None, ALU.mult),
                waits=[("VE", ck_w4), ("VE", ck_dtm)] if (j == 0 and ei == 0) else [])
    ve_dwtm = p.op("vector", lambda e: e.drain(), inc="VE")
    p4a_mms(12)
    p4a_evac(11)
    p4a_mms(13)
    PSdw = PSB[:].bitcast(BF16)[0:15, 0:2048]
    for j in range(TCHUNK):
        p.op("tensor", lambda e, j=j: e.transpose(
            PSdw[:, j * 128:(j + 1) * 128], dwtm[:, j * 16: j * 16 + 15],
            identb[:, :]),
            waits=[("VE", ve_dwtm)] if j == 0 else [],
            inc="PE" if j == TCHUNK - 1 else None)
    ck_dw2_pe = p.c("PE")
    downw = ggs[0:15, 2048:4096]
    p.op("vector", lambda e: e.tensor_copy(downw, PSdw),
         waits=[("PE", ck_dw2_pe)])
    p.op("vector", lambda e: e.drain())
    p.op("vector", lambda e: e.tensor_tensor(
        downw, downw, ff[0:15, :], ALU.mult))
    ck_downw = p.op("vector", lambda e: e.drain(), inc="VE")
    p4a_evac(12)
    p4a_mms(14)
    p4a_evac(13)
    p4a_mms(15)
    p4a_evac(14)
    p4a_evac(15)

    bt = {}
    for u in range(16):
        oc, h = u // 2, u % 2
        bt[u] = p.op("vector", lambda e, oc=oc, h=h: e.tensor_tensor(
            gg[:, oc * T + h * 1024: oc * T + (h + 1) * 1024],
            gg[:, oc * T + h * 1024: oc * T + (h + 1) * 1024],
            ff[:, h * 1024:(h + 1) * 1024], ALU.mult),
            waits=[("AC", p4a_ev[u])], inc="VE")
    ck_g_sil = {}
    for u in range(16):
        oc, h = u // 2, u % 2
        PU = PSA[:, h * 1024:(h + 1) * 1024]
        for j in range(2):
            w = []
            if j == 0:
                w = [("VE", ck_downw)]
                if u < 2:
                    w += [("WU", ck_wu), ("AC", p4a_ev[14 + h])]
                if u >= 2:
                    w.append(("AC", ck_g_sil[u - 2]))
            p.op("tensor", lambda e, oc=oc, h=h, j=j, PU=PU: e.matmul(
                PU[:, j * 512:(j + 1) * 512],
                wu_bf[:, oc * 128:(oc + 1) * 128],
                downw[:, h * 1024 + j * 512: h * 1024 + (j + 1) * 512],
                start=True, stop=False), waits=w)
        for j in range(2):
            p.op("tensor", lambda e, oc=oc, h=h, j=j, PU=PU: e.matmul(
                PU[:, j * 512:(j + 1) * 512], identb[:],
                gg[:, oc * T + h * 1024 + j * 512:
                    oc * T + h * 1024 + (j + 1) * 512],
                start=False, stop=True),
                waits=[("VE", bt[u])] if j == 0 else [],
                inc="PE" if j == 1 else None)
        pe_u = p.c("PE")
        ck_g_sil[u] = p.op("scalar", lambda e, oc=oc, h=h, PU=PU: e.activation(
            gg[:, oc * T + h * 1024: oc * T + (h + 1) * 1024], PU, AF.Silu),
            waits=[("PE", pe_u)], inc="AC")
    ck_g_done_ac = p.c("AC")
    ck_g_done_ve = p.c("VE")

    SSs = [PSA[:, 0:1024], PSA[:, 1024:2048]]
    AVs = [PSB[:, 0:1024], PSB[:, 1024:2048]]
    Pps = [ggs[:, 4096:5120], ggs[:, 5120:6144],
           wbf[:, 4096:5120], wbf[:, 5120:6144]]
    wbfF = wbf[:].bitcast(F32)
    strips = [wbfF[0:65, 0:1024], wbfF[0:65, 1024:2048]]
    qaug = [ggs[0:96, 0:2048], ggs[0:96, 2048:4096]]
    kaug = [hh[0:96, 0:2048], hh[0:96, 2048:4096]]
    ve_pad = 0
    for b in range(2):
        p.op("vector", lambda e, b=b: e.memset(
            ggs[64:96, b * 2048:(b + 1) * 2048], 0.0),
            waits=[("PE", ck_lu_pe), ("VE", ck_downw)] if b == 0 else [])
        p.op("vector", lambda e, b=b: e.memset(
            hh[64:96, b * 2048:(b + 1) * 2048], 0.0),
            waits=[("PE", ck_dw2_pe)] if b == 0 else [])
        p.op("vector", lambda e: e.drain())
        ve_pad = p.op("vector", lambda e, b=b: e.memset(
            hh[64:65, b * 2048:(b + 1) * 2048], 1.0), inc="VE")

    qk_pe_after_head = {}
    build_ve = {}
    build_gd = {}
    evac_ve = {}
    lazy_tt_ve = {}
    lazy_gd1 = {}
    pe_av = {}
    ac_exp = {}
    ck_wo = 0
    g_iter = 0
    W71 = 1071
    ffb = ff[:].bitcast(BF16)
    evac_ac = {}
    strip_users = [[], []]
    pending_av = []
    lazyq = []
    lazy_done_ve = {}
    nonlocal_po = [0]
    ve_sh0 = 0

    def flush_one_av():
        g, hx, t, AV, Pp, w_av = pending_av.pop(0)
        for j in range(2):
            p.op("tensor", lambda e, hx=hx, t=t, j=j, AV=AV, Pp=Pp: e.matmul(
                AV[0:96, j * 512:(j + 1) * 512],
                vv[:, t * W71 + 65 * hx: t * W71 + 65 * hx + 96],
                Pp[:, j * 512:(j + 1) * 512],
                start=(t == 0), stop=(t == TCHUNK - 1)),
                waits=w_av if j == 0 else [], inc="PE" if j == 1 else None)
        pe_av[g] = p.c("PE")

    def flush_to(depth):
        while len(pending_av) > depth:
            flush_one_av()

    def emit_builds(h, w_extra, w_q=()):
        par = h % 2
        hc, hr = h // 2, 64 * (h % 2)
        build_gd[h] = p.op("gpsimd", lambda e, par=par, h=h: e.dma_start(
            out=qaug[par][64:65, :], in_=scr_neg[h:h + 1, :]),
            waits=[("BN", ck_negc), ("VE", ve_pad)] + w_extra,
            inc=f"NG{par}", amt=16)
        p.op("vector", lambda e, par=par, hr=hr, hc=hc: e.tensor_copy(
            kaug[par][0:64, :], kk[hr:hr + 64, hc * T:(hc + 1) * T]),
            waits=w_extra + [("VE", ve_pad)])
        p.op("vector", lambda e, par=par, hr=hr, hc=hc: e.tensor_copy(
            qaug[par][0:64, :], qq[hr:hr + 64, hc * T:(hc + 1) * T]),
            waits=list(w_q))
        build_ve[h] = p.op("vector", lambda e: e.drain(), inc="VE")

    def emit_lazy_front(ei, hx, sh):
        si = ei % 4
        L = f"L{si}"
        lazy_gd1[si] = p.op("gpsimd", lambda e, si=si, ei=ei: e.dma_start(
            out=scr_lin[si:si + 1, :], in_=strips[ei % 2][64:65, :]),
            waits=[("VE", evac_ac[(hx, sh)])], inc=L, amt=16)
        p.op("gpsimd", lambda e, si=si: e.dma_start(
            out=lrtm[:, si * 8:(si + 1) * 8],
            in_=scr_lin[si:si + 1, :].rearrange("o (pp c) -> (o pp) c", pp=128)),
            waits=[(L, lazy_gd1[si])], inc=L, amt=16)
        lazyq.append((ei, hx, sh, si, p.c(L)))

    def emit_lazy_back():
        ei, hx, sh, si, gd2 = lazyq.pop(0)
        hc, hr = hx // 2, 64 * (hx % 2)
        L = f"L{si}"
        p.op("vector", lambda e, si=si: e.reciprocal(
            lrtm[:, si * 8:(si + 1) * 8], lrtm[:, si * 8:(si + 1) * 8]),
            waits=[(L, gd2)])
        p.op("vector", lambda e: e.drain())
        p.op("vector", lambda e, si=si: e.tensor_copy(
            lrtb[:, si * 8:(si + 1) * 8], lrtm[:, si * 8:(si + 1) * 8]))
        ve_rc = p.op("vector", lambda e: e.drain(), inc="VE")
        p.op("gpsimd", lambda e, si=si: e.dma_start(
            out=scr_li2[si:si + 1, :].rearrange("o (pp c) -> (o pp) c", pp=128),
            in_=lrtb[:, si * 8:(si + 1) * 8]),
            waits=[("VE", ve_rc)], inc=L, amt=16)
        gd3 = p.c(L)
        lb = ffb[:, si * 1024:(si + 1) * 1024]
        p.op("gpsimd", lambda e, si=si, lb=lb: e.dma_start(
            out=lb, in_=scr_li2[si:si + 1, :].broadcast_to([128, 1024])),
            waits=[(L, gd3), ("PO", lazy_done_ve.get(si, 0))],
            inc=L, amt=16)
        gd4 = p.c(L)
        lazy_done_ve[si] = p.op(
            "gpsimd", lambda e, lb=lb, hr=hr, hc=hc, sh=sh: e.tensor_tensor(
                xq[hr:hr + 64, hc * T + sh * 1024: hc * T + sh * 1024 + 1024],
                xq[hr:hr + 64, hc * T + sh * 1024: hc * T + sh * 1024 + 1024],
                lb[hr:hr + 64, :], ALU.mult),
            waits=[(L, gd4), ("VE", evac_ve[(hx, sh)])], inc="PO")
        if sh == 0:
            nonlocal_po[0] = p.c("PO")

    emit_builds(0, [], [("AC", ck_g_done_ac)])
    emit_builds(1, [], [("AC", ck_g_done_ac)])
    ei_count = 0
    for hp in range(NH // 2):
        h0, h1 = 2 * hp, 2 * hp + 1
        for sh in range(2):
            for t in range(TCHUNK):
                for hx in (h0, h1):
                    par = hx % 2
                    AV = AVs[par]
                    SSt = SSs[g_iter % 2]
                    Pp = Pps[g_iter % 4]
                    flush_to(3)
                    w = [("AC", ac_exp.get(g_iter - 2, 0))]
                    if g_iter < 2:
                        w.append(("AC", ck_g_done_ac))
                    if t == 0 and sh == 0:
                        w += [("VE", build_ve[hx]), (f"NG{par}", build_gd[hx])]
                    if g_iter < 2:
                        w += [("AC", ck_g_done_ac), ("VE", ck_g_done_ve)]
                    for j in range(2):
                        p.op("tensor",
                             lambda e, par=par, sh=sh, t=t, j=j, SSt=SSt: e.matmul(
                                 SSt[:, j * 512:(j + 1) * 512],
                                 kaug[par][0:96, t * 128:(t + 1) * 128],
                                 qaug[par][0:96,
                                           sh * 1024 + j * 512: sh * 1024 + (j + 1) * 512],
                                 start=True, stop=True),
                             waits=w if j == 0 else [],
                             inc="PE" if j == 1 else None)
                    pe_qk = p.c("PE")
                    if sh == 1 and t == TCHUNK - 1:
                        qk_pe_after_head[hx] = pe_qk
                    ac_exp[g_iter] = p.op(
                        "scalar", lambda e, SSt=SSt, Pp=Pp: e.activation(
                            Pp, SSt, AF.Exp, scale=8.0),
                        waits=[("PE", pe_qk), ("PE", pe_av.get(g_iter - 4, 0))],
                        inc="AC")
                    w_av = [("AC", ac_exp[g_iter])]
                    if t == 0:
                        w_av.append(("VE", evac_ac.get((hx, sh - 1), 0)))
                        if sh == 0:
                            w_av.append(("VE", evac_ac.get((hx - 2, 1), 0)))
                    pending_av.append((g_iter, hx, t, AV, Pp, w_av))
                    g_iter += 1
            flush_to(0)
            if sh == 1 and hp < NH // 2 - 1:
                emit_builds(h0 + 2, [("PE", qk_pe_after_head[h0])])
                emit_builds(h1 + 2, [("PE", qk_pe_after_head[h1])])
            for hx in (h0, h1):
                par = hx % 2
                AV = AVs[par]
                hc, hr = hx // 2, 64 * (hx % 2)
                pe_last = pe_av[g_iter - 1 - (1 - (hx - h0))]
                si = ei_count % 4
                s2 = ei_count % 2
                p.op("vector", lambda e, AV=AV, s2=s2: e.tensor_copy(
                    strips[s2], AV[0:65, :]),
                    waits=[("PE", pe_last)] + strip_users[s2] +
                          ([("VE", ck_g_done_ve)] if g_iter <= 40 else []))
                evac_ac[(hx, sh)] = p.op("vector", lambda e: e.drain(),
                                         inc="VE")
                p.op("vector", lambda e, s2=s2, hr=hr, hc=hc, sh=sh: e.tensor_copy(
                    xq[hr:hr + 64,
                       hc * T + sh * 1024: hc * T + sh * 1024 + 1024],
                    strips[s2][0:64, :]))
                evac_ve[(hx, sh)] = p.op("vector", lambda e: e.drain(),
                                         inc="VE")
                if sh == 0:
                    ve_sh0 = evac_ve[(hx, sh)]
                emit_lazy_front(ei_count, hx, sh)
                strip_users[s2] = [("VE", evac_ve[(hx, sh)]),
                                   (f"L{si}", lazy_gd1[si])]
                ei_count += 1
                if len(lazyq) > 2:
                    emit_lazy_back()
            if hp == NH // 2 - 1 and sh == 0:
                while lazyq:
                    emit_lazy_back()
        if hp == 3:
            for wc in range(NCHUNK):
                ck_wo = p.op("gpsimd", lambda e, wc=wc: e.dma_start(
                    out=qq[:, wc * 1024:(wc + 1) * 1024],
                    in_=wo[wc * 128:(wc + 1) * 128, :]),
                    waits=[("PE", qk_pe_after_head[7]), ("VE", build_ve[7])]
                    if wc == 0 else [], inc="WO", amt=16)
    ck_attn_ac = p.c("AC")
    tail = []
    while lazyq:
        ei, hx, sh, si, gd2 = lazyq.pop(0)
        p.op("vector", lambda e, si=si: e.reciprocal(
            lrtm[:, si * 8:(si + 1) * 8], lrtm[:, si * 8:(si + 1) * 8]),
            waits=[(f"L{si}", gd2)])
        p.op("vector", lambda e: e.drain())
        p.op("vector", lambda e, si=si: e.tensor_copy(
            lrtb[:, si * 8:(si + 1) * 8], lrtm[:, si * 8:(si + 1) * 8]))
        ve_rc = p.op("vector", lambda e: e.drain(), inc="VE")
        tail.append((ei, hx, sh, si, ve_rc))
    tail2 = []
    for ei, hx, sh, si, ve_rc in tail:
        p.op("gpsimd", lambda e, si=si: e.dma_start(
            out=scr_li2[si:si + 1, :].rearrange("o (pp c) -> (o pp) c", pp=128),
            in_=lrtb[:, si * 8:(si + 1) * 8]),
            waits=[("VE", ve_rc)], inc=f"L{si}", amt=16)
        gd3 = p.c(f"L{si}")
        tail2.append((ei, hx, sh, si, gd3))
    tail3 = []
    for ei, hx, sh, si, gd3 in tail2:
        lb = ffb[:, si * 1024:(si + 1) * 1024]
        p.op("gpsimd", lambda e, si=si, lb=lb: e.dma_start(
            out=lb, in_=scr_li2[si:si + 1, :].broadcast_to([128, 1024])),
            waits=[(f"L{si}", gd3), ("PO", lazy_done_ve.get(si, 0))],
            inc=f"L{si}", amt=16)
        tail3.append((ei, hx, sh, si, p.c(f"L{si}")))
    for ei, hx, sh, si, gd4 in tail3:
        hc, hr = hx // 2, 64 * (hx % 2)
        lb = ffb[:, si * 1024:(si + 1) * 1024]
        lazy_done_ve[si] = p.op(
            "gpsimd", lambda e, lb=lb, hr=hr, hc=hc, sh=sh: e.tensor_tensor(
                xq[hr:hr + 64, hc * T + sh * 1024: hc * T + sh * 1024 + 1024],
                xq[hr:hr + 64, hc * T + sh * 1024: hc * T + sh * 1024 + 1024],
                lb[hr:hr + 64, :], ALU.mult),
            waits=[(f"L{si}", gd4), ("VE", evac_ve[(hx, sh)])], inc="PO")
        if sh == 0:
            nonlocal_po[0] = p.c("PO")
    p.op("vector", lambda e: e.drain(), inc="VE")
    ck_attn_ve = p.c("VE")
    ck_attn_po = p.c("PO")

    po_sh0 = nonlocal_po[0]
    x_in = [kf[:, 0:2048], kf[:, 2048:4096]]
    out_sb = [kf[:, 4096:6144], kf[:, 6144:8192]]
    wo_view = lambda kc, oc: qq[:, kc * 1024 + oc * 128: kc * 1024 + oc * 128 + 128]
    ck_comb = {}
    ck_mult = {}
    ck_xin = {}
    pe_wo = {}

    def p7_mm_block(oc, js):
        PP = PSA if oc % 2 == 0 else PSB
        for j in js:
            for kc in range(NCHUNK):
                w2 = []
                if kc == 0:
                    if oc == 0 and j == 0:
                        w2 = [("WO", ck_wo), ("PO", po_sh0), ("VE", ve_sh0),
                              ("AC", ck_attn_ac)]
                    if oc == 1 and j == 0:
                        w2.append(("VE", evac_ac[(NH - 1, 1)]))
                    if oc == 0 and j == 2:
                        w2 += [("PO", ck_attn_po), ("VE", ck_attn_ve)]
                    if oc >= 2 and j == 0:
                        w2.append(("VE", ck_mult[oc - 2]))
                p.op("tensor", lambda e, oc=oc, kc=kc, j=j, PP=PP: e.matmul(
                    PP[:, j * 512:(j + 1) * 512], wo_view(kc, oc),
                    xq[:, kc * T + j * 512: kc * T + (j + 1) * 512],
                    start=(kc == 0), stop=(kc == NCHUNK - 1)),
                    waits=w2,
                    inc="PE" if (kc == NCHUNK - 1 and j == js[-1]) else None)
        pe_wo[oc] = p.c("PE")

    for ocp in range(0, NCHUNK, 2):
        for oc in (ocp, ocp + 1):
            wx = [("VE", build_ve[NH - 1])]
            if oc >= 2:
                wx.append(("VE", ck_comb[oc - 2]))
            ck_xin[oc] = p.op("sync", lambda e, oc=oc: e.dma_start(
                out=x_in[oc % 2], in_=xT[oc * 128:(oc + 1) * 128, :]),
                waits=wx, inc="XA" if oc % 2 == 0 else "XB", amt=16)
        p7_mm_block(ocp, (0, 1))
        p7_mm_block(ocp + 1, (0, 1))
        p7_mm_block(ocp, (2, 3))
        p7_mm_block(ocp + 1, (2, 3))
        for oc in (ocp, ocp + 1):
            PP = PSA if oc % 2 == 0 else PSB
            p.op("vector", lambda e, oc=oc, PP=PP: e.tensor_tensor(
                out_sb[oc % 2], PP[:], gg[:, oc * T:(oc + 1) * T], ALU.mult),
                waits=[("PE", pe_wo[oc])] +
                      ([("ST", oc * 16)] if oc >= 2 else []))
            ck_mult[oc] = p.op("vector", lambda e: e.drain(), inc="VE")
            ck_comb[oc] = p.op("vector", lambda e, oc=oc: e.tensor_tensor(
                out_sb[oc % 2], out_sb[oc % 2], x_in[oc % 2], ALU.add),
                waits=[("XA" if oc % 2 == 0 else "XB", ck_xin[oc])],
                inc="VE")
            p.op("sync", lambda e, oc=oc: e.dma_start(
                out=outT[oc * 128:(oc + 1) * 128, :], in_=out_sb[oc % 2]),
                waits=[("VE", ck_comb[oc])], inc="ST", amt=16)
    p.op("sync", lambda e: e.wait_ge(sems["ST"], NCHUNK * 16))

    with nc.Block() as block:
        def replay(eng_name):
            ops = p.s[eng_name]

            def body(eng):
                last = {}
                for fn, waits, inc, amt in ops:
                    for s, v in waits:
                        if last.get(s, 0) < v:
                            eng.wait_ge(sems[s], v)
                            last[s] = v
                    inst = fn(eng)
                    if inc is not None and inst is not None:
                        inst.then_inc(sems[inc], amt)
            return body

        block.sync(replay("sync"))
        block.gpsimd(replay("gpsimd"))
        block.tensor(replay("tensor"))
        block.scalar(replay("scalar"))
        block.vector(replay("vector"))

    es.close()
    return nc


def _host_constants():
    base = 10000.0
    inv_freq = 1.0 / base ** (np.arange(0, HD, 2, dtype=np.float64) / HD)
    t = np.arange(T, dtype=np.float64)
    freqs = np.outer(t, inv_freq)
    emb = np.concatenate([freqs, freqs], axis=-1)
    cos = np.cos(emb).astype(np.float32).T
    sin = np.sin(emb).astype(np.float32).T
    cos2 = np.ascontiguousarray(np.concatenate([cos, cos], axis=0))
    sin2 = np.ascontiguousarray(np.concatenate([sin, sin], axis=0))
    R = np.zeros((HD, HD), np.float32)
    for i in range(32):
        R[i, 2 * i + 1] = -1.0
    for i in range(32, 64):
        R[i, 2 * (i - 32)] = 1.0
    RT = R.T
    rt2 = np.zeros((128, 128), np.float32)
    rt2[0:64, 0:64] = RT
    rt2[64:128, 64:128] = RT
    ident = np.eye(128, dtype=np.float32)
    ehm = np.zeros((D, NH), np.float32)
    for d in range(D):
        ehm[d, d // HD] = 1.0
    return cos2, sin2, rt2, ident, ehm


def kernel(**inputs):
    f32 = lambda a: np.ascontiguousarray(np.asarray(a, dtype=np.float32))
    x = f32(inputs["L0_attn_hidden_states"])
    wq = f32(inputs["L0_attn_Wq"]).T.copy()
    wk = f32(inputs["L0_attn_Wk"]).T.copy()
    wv = f32(inputs["L0_attn_Wv"]).T.copy()
    wo = f32(inputs["L0_attn_Wo"]).T.copy()
    wff = f32(inputs["L0_ffn_W"]).T.copy()
    wp = f32(inputs["L0_ffn_Wp"]).T.copy()
    bp = f32(inputs["L0_ffn_bp"]).reshape(1, 256)
    simT = f32(inputs["L0_ffn_sim"]).T.copy()
    temp = f32(inputs["L0_ffn_temp"]).reshape(1, 1)
    wd_cat = np.concatenate([f32(inputs[f"L0_ffn_Wd{i}"]) for i in range(4)],
                            axis=0)
    wd = wd_cat.T.copy()
    wu_cat = np.concatenate([f32(inputs[f"L0_ffn_Wu{i}"]) for i in range(4)],
                            axis=1)
    wu = wu_cat.T.copy()
    cos2, sin2, rt2, ident, ehm = _host_constants()

    shared = {"wq": wq, "wk": wk, "wv": wv, "wo": wo, "wff": wff, "wp": wp,
              "wd": wd, "wu": wu, "bp": bp, "simT": simT, "temp": temp,
              "cos2": cos2, "sin2": sin2, "rt2": rt2, "ident": ident,
              "eh": ehm}
    in_maps = []
    for b in range(8):
        m = dict(shared)
        m["xT"] = np.ascontiguousarray(x[b].T)
        in_maps.append(m)

    nc = build_nc()
    res = run_bass_kernel_spmd(nc, in_maps, core_ids=list(range(8)))
    out = np.stack([np.asarray(r["outT"], np.float32).T for r in res.results],
                   axis=0)
    return np.ascontiguousarray(out)


if __name__ == "__main__":
    import reference as ref
    inputs = ref.setup_inputs()
    got = kernel(**{k: np.asarray(v) for k, v in inputs.items()})
    want = np.asarray(ref.reference(**inputs))
    num = float(np.mean((got - want) ** 2))
    den = float(np.mean(want ** 2))
    print("resvar:", num / den)
